# version 15
# baseline (speedup 1.0000x reference)
import math
import numpy as np
import concourse.bass as bass
import concourse.mybir as mybir
from concourse.bass_utils import run_bass_kernel_spmd

F32 = mybir.dt.float32
BF16 = mybir.dt.bfloat16
U8 = mybir.dt.uint8
I16 = mybir.dt.int16
AF = mybir.ActivationFunctionType
ALU = mybir.AluOpType
AX = mybir.AxisListType

D = 2048
T = 4096
NQ = 1024
NH = 16
VD = 128
NF = 16384
LAMBDA_INIT = 0.8 - 0.6 * math.exp(-0.3)
ONE_M_LI = 1.0 - LAMBDA_INIT
GC1 = 0.7978845608028654
GC3 = 0.7978845608028654 * 0.044715
EPS = 1e-5

SQ_GC3 = math.sqrt(GC3)
CONST_VALS = [0.0, 1.0, 0.125, -12.0, 0.5, -0.25, EPS, 1.0 / D, 1.0 / VD,
              -0.5, math.log(ONE_M_LI), GC1, GC3, LAMBDA_INIT, 0.0625,
              -0.499999, 63.5, -1.0 / 256.0, SQ_GC3]

N_CORES = 8
TILES = [[c4, 4 + c4, 11 - c4, 15 - c4] for c4 in range(4)]
PC = [8, 16, 24, 32]
MOFF = [0, 8, 24, 48]
NCH_TOT = 80



def _nperm():
    p = []
    for b in range(NF // 256):
        p.extend(range(256 * b, 256 * (b + 1), 2))
        p.extend(range(256 * b + 1, 256 * (b + 1), 2))
    return np.array(p)


def _pack_u8(a):
    a = np.ascontiguousarray(np.asarray(a, np.int32))
    return np.ascontiguousarray(a.view(np.uint8).reshape(*a.shape, 4)[..., 0])


def _chunkmajor(w):
    Dr, C = w.shape
    return np.ascontiguousarray(w.reshape(Dr // 128, 128, C).transpose(1, 0, 2)
                                .reshape(128, (Dr // 128) * C))


def _host_inputs(inputs):
    x = np.asarray(inputs["L0_attn_x"], np.float32).reshape(2, T, D)
    wqkv = np.asarray(inputs["L0_attn_Wqkv"], np.float32)
    wproj = np.asarray(inputs["L0_attn_Wproj"], np.float32)

    kcols, vcols, qcols = [], [], []
    for h in range(NH):
        kcols.extend(range(1024 + 64 * h, 1024 + 64 * h + 64))
        kcols.extend(range(3072 + 64 * h, 3072 + 64 * h + 64))
    for h in range(NH):
        vcols.extend(range(4096 + 128 * h, 4096 + 128 * h + 128))
    for h in range(NH):
        qcols.extend(range(64 * h, 64 * h + 64))
        qcols.extend(range(2048 + 64 * h, 2048 + 64 * h + 64))
    wqkv_k = _chunkmajor(np.ascontiguousarray(wqkv[:, kcols]))
    wqkv_v = _chunkmajor(np.ascontiguousarray(wqkv[:, vcols]))
    wqkv_q = _chunkmajor(np.ascontiguousarray(wqkv[:, qcols]))
    wproj_cm = _chunkmajor(np.ascontiguousarray(wproj))

    lamv = np.zeros((128, 256), np.float32)
    lamv[:, 0:64] = np.asarray(inputs["L0_attn_lambda_q1"], np.float32)
    lamv[:, 64:128] = np.asarray(inputs["L0_attn_lambda_q2"], np.float32)
    lamv[:, 128:192] = np.asarray(inputs["L0_attn_lambda_k1"], np.float32)
    lamv[:, 192:256] = np.asarray(inputs["L0_attn_lambda_k2"], np.float32)

    import ml_dtypes
    consts = np.zeros((128, 20), np.float32)
    consts[:, :len(CONST_VALS)] = CONST_VALS
    ident = np.eye(128, dtype=ml_dtypes.bfloat16)

    perm = _nperm()

    def _gu_blocks(pk_t):
        a = pk_t.reshape(8, 128, 32, 512)
        return np.ascontiguousarray(a.transpose(2, 1, 0, 3).reshape(32, 128, 4096))

    gate_pk3 = _gu_blocks(_pack_u8(inputs["L0_ffn_gate_W_packed"])[perm, :].T)
    up_pk3 = _gu_blocks(_pack_u8(inputs["L0_ffn_up_W_packed"])[perm, :].T)
    d8 = _pack_u8(inputs["L0_ffn_down_W_packed"]).T
    down_pk3 = np.ascontiguousarray(
        d8.reshape(32, 2, 128, 2, 1024).transpose(0, 2, 1, 3, 4)
        .reshape(32, 128, 4096))
    gsc_t = np.asarray(inputs["L0_ffn_gate_scales"], np.float32)[perm, :].T
    usc_t = np.asarray(inputs["L0_ffn_up_scales"], np.float32)[perm, :].T
    gu_sc = np.ascontiguousarray(np.concatenate([gsc_t, usc_t], 0))
    dsc_t = np.asarray(inputs["L0_ffn_down_scales"], np.float32).T
    ds_sc = np.ascontiguousarray(
        dsc_t.reshape(4, 128, 2048).transpose(1, 0, 2).reshape(128, 8192))

    shared = {
        "wqkv_k": wqkv_k, "wqkv_v": wqkv_v, "wqkv_q": wqkv_q, "wproj": wproj_cm,
        "lamv": lamv, "consts": consts, "ident": ident,
        "gate_pk3": gate_pk3, "up_pk3": up_pk3, "down_pk3": down_pk3,
        "gu_sc": gu_sc, "ds_sc": ds_sc,
    }

    kt = np.arange(128)
    qt = np.arange(256)
    in_maps = []
    for c in range(N_CORES):
        b, c4 = divmod(c, 4)
        tiles = TILES[c4]
        m = np.zeros((32, 128, 512), ml_dtypes.bfloat16)
        for jt in range(4):
            for i8 in range(8):
                kc = PC[jt] - 8 + i8
                m1 = ((128 * kc + kt[:, None])
                      <= (256 * tiles[jt] + qt[None, :256])).astype(np.float32)
                m[jt * 8 + i8] = np.concatenate([m1, m1], 1)
        d = dict(shared)
        d["x_kv"] = np.ascontiguousarray(x[b])
        d["x_q"] = np.ascontiguousarray(
            np.concatenate([x[b, 256 * t:256 * (t + 1)] for t in tiles], 0))
        d["masks"] = np.ascontiguousarray(m.transpose(1, 0, 2).reshape(128, 32 * 512))
        in_maps.append(d)
    return in_maps



def _aslist(w):
    if w is None:
        return []
    if isinstance(w, tuple) and len(w) == 2 and isinstance(w[0], str):
        return [w]
    out = []
    for m in w:
        out.extend(_aslist(m))
    return out


class Prog:
    ENGINES = ("tensor", "vector", "scalar", "gpsimd", "sync")

    def __init__(self, nc, ctx):
        self.nc = nc
        self.ctx = ctx
        self.body = {e: [] for e in self.ENGINES}
        self.sems = {}
        self.waited = {e: {} for e in self.ENGINES}

    def sem(self, name):
        if name not in self.sems:
            h = self.ctx.enter_context(self.nc.semaphore(name))
            self.sems[name] = [h, 0]
        return self.sems[name]

    def wait(self, eng, mark):
        for m in _aslist(mark):
            if m[1] <= 0 or self.waited[eng].get(m[0], 0) >= m[1]:
                continue
            self.waited[eng][m[0]] = m[1]
            h = self.sems[m[0]][0]
            self.body[eng].append(lambda e, h=h, v=m[1]: e.wait_ge(h, v))

    def op(self, eng, fname, *a, inc=None, **k):
        if inc is not None:
            s = self.sem(inc[0])
            s[1] += inc[1]
            h, val = s[0], s[1]

            def run(e, fname=fname, a=a, k=k, h=h, amt=inc[1]):
                getattr(e, fname)(*a, **k).then_inc(h, amt)
            self.body[eng].append(run)
            return (inc[0], val)

        def run(e, fname=fname, a=a, k=k):
            getattr(e, fname)(*a, **k)
        self.body[eng].append(run)
        return None

    def mark(self, name):
        return (name, self.sems[name][1]) if name in self.sems else (name, 0)


SELF_SERIAL = True


class SerialEngine:
    def __init__(self, prog, eng, semname, serial=True):
        self.p, self.eng, self.semname = prog, eng, semname
        self.serial = serial
        self.p.sem(semname)

    def __init2__(self):
        pass

    def op(self, fname, *a, waits=(), noserial=False, **k):
        self.p.wait(self.eng, waits)
        if self.serial and not noserial:
            n = self.p.sems[self.semname][1]
            if n > 0:
                self.p.wait(self.eng, (self.semname, n))
        return self.p.op(self.eng, fname, *a, inc=(self.semname, 1), **k)

    def mark(self):
        return self.p.mark(self.semname)


class DmaQueue:
    def __init__(self, prog, eng):
        self.p, self.eng = prog, eng

    def dma(self, semname, out, in_, waits=(), transpose=False):
        self.p.wait(self.eng, waits)
        n = self.p.sems[semname][1] if semname in self.p.sems else 0
        if n > 0:
            self.p.wait(self.eng, (semname, n))
        if transpose:
            return self.p.op(self.eng, "dma_start_transpose", out, in_,
                             inc=(semname, 16))
        return self.p.op(self.eng, "dma_start", out=out, in_=in_,
                         inc=(semname, 16))

    def dma_group(self, semname, pairs, waits=()):
        self.p.wait(self.eng, waits)
        for out, in_ in pairs:
            self.p.op(self.eng, "dma_start", out=out, in_=in_, inc=(semname, 16))
        return self.p.mark(semname)


class TensorEng:
    def __init__(self, prog):
        self.p = prog
        self.rot = 0
        self.p.sem("st0")
        self.p.sem("st1")

    def wait(self, mark):
        self.p.wait("tensor", mark)

    def mm(self, *a, **k):
        self.p.op("tensor", "matmul", *a, **k)

    def mile(self, fname, *a, **k):
        name = f"st{self.rot}"
        self.rot ^= 1
        n = self.p.sems[name][1]
        if n > 0:
            self.p.wait("tensor", (name, n))
        return self.p.op("tensor", fname, *a, inc=(name, 1), **k)

    def fence(self):
        return [self.p.mark("st0"), self.p.mark("st1")]



def build_and_run(in_maps, trace=False):
    from contextlib import ExitStack
    nc = bass.Bass("TRN2", target_bir_lowering=False)

    di = lambda n, s, dt: nc.dram_tensor(n, s, dt, kind="ExternalInput")
    dd = lambda n, s, dt: nc.dram_tensor(n, s, dt, kind="Internal")
    x_kv = di("x_kv", [T, D], F32)
    x_q = di("x_q", [NQ, D], F32)
    wqkv_k = di("wqkv_k", [128, 16 * 2048], F32)
    wqkv_v = di("wqkv_v", [128, 16 * 2048], F32)
    wqkv_q = di("wqkv_q", [128, 16 * 2048], F32)
    wproj = di("wproj", [128, 16 * 2048], F32)
    lamv = di("lamv", [128, 256], F32)
    consts = di("consts", [128, 20], F32)
    ident = di("ident", [128, 128], BF16)
    masks = di("masks", [128, 32 * 512], BF16)
    gate_pk3 = di("gate_pk3", [32, 128, 4096], U8)
    up_pk3 = di("up_pk3", [32, 128, 4096], U8)
    down_pk3 = di("down_pk3", [32, 128, 4096], U8)
    gu_sc = di("gu_sc", [128, NF], F32)
    ds_sc = di("ds_sc", [128, 8192], F32)
    out = nc.dram_tensor("out", [NQ, D], F32, kind="ExternalOutput")

    xn_kv_d = dd("xn_kv_d", [T, D], BF16)
    xn_q_d = dd("xn_q_d", [NQ, D], BF16)
    xnqT_d = dd("xnqT_d", [D, NQ], BF16)
    k12T_d = dd("k12T_d", [NH, 128, T], BF16)
    vext_d = dd("vext_d", [NH, 128, 32, 132], BF16)
    q12T_d = dd("q12T_d", [NH, 128, NQ], BF16)
    s_d = dd("s_d", [NQ, D], F32)
    attnT_d = dd("attnT_d", [8, 128, NH, 128], BF16)
    inter_d = dd("inter_d", [128, 128, NQ], BF16)
    wdT_dram = dd("wdT_dram", [32, 128, 8192], BF16)

    ctx = ExitStack()
    with ctx:
        sb = lambda n, s, dt: ctx.enter_context(nc.sbuf_tensor(n, s, dt))
        psal = lambda n, s, dt: ctx.enter_context(nc.psum_tensor(n, s, dt))

        AR0 = sb("AR0", [128, 65536], U8)
        AR1 = sb("AR1", [128, 65536], U8)
        AR2 = sb("AR2", [128, 49152], U8)
        AR3 = sb("AR3", [128, 22528], U8)

        def v8(ar, off, nbytes, dt, cols):
            return ar[:, off:off + nbytes].bitcast(dt)[:, 0:cols]

        wk_sb = v8(AR0, 0, 65536, BF16, 16 * 2048)
        wv_sb = wk_sb
        wq_sb = wk_sb
        wp_sb = wk_sb

        xt = [v8(AR1, 0, 8192, F32, D), v8(AR1, 8192, 8192, F32, D)]
        lnsq = v8(AR1, 16384, 8192, F32, D)
        xnt = [v8(AR1, 24576, 4096, BF16, D), v8(AR1, 28672, 4096, BF16, D)]
        xnT_sb = [v8(AR2, 0, 16384, BF16, 16 * 512),
                  v8(AR2, 16384, 16384, BF16, 16 * 512)]
        qkv_ev = [v8(AR1, 0, 1024, BF16, 512), v8(AR1, 1024, 1024, BF16, 512)]
        vtr_ev = [v8(AR1, 2048, 1024, BF16, 512), v8(AR1, 3072, 1024, BF16, 512)]
        vex_ev = [v8(AR1, 4096, 1088, BF16, 4 * 132), v8(AR1, 5248, 1088, BF16, 4 * 132)]

        k12 = [v8(AR2, 0, 8192, BF16, T), v8(AR2, 8192, 8192, BF16, T)]
        vex = [v8(AR2, 16384, 8448, BF16, 32 * 132), v8(AR2, 24832, 8448, BF16, 32 * 132)]
        q12 = [v8(AR2, 33280, 2048, BF16, NQ), v8(AR2, 35328, 2048, BF16, NQ)]
        att_qs = [v8(AR2, 0, 4096, BF16, D), v8(AR2, 4096, 4096, BF16, D)]
        s_ev = [v8(AR2, 8192, 8192, F32, D)]

        cst = sb("cst", [128, 20], F32)
        for i, v in enumerate(CONST_VALS):
            nc.const_aps.register(F32, v, cst[:, i:i + 1])
        lam_sb = sb("lam_sb", [128, 8], F32)
        lamv_sb = v8(AR1, 16384, 1024, F32, 256)
        lamx = v8(AR1, 20480, 1024, F32, 256)
        id_b = sb("id_b", [128, 128], BF16)
        lnst = sb("lnst", [128, 8], F32)
        ep_st = sb("ep_st", [128, 16], F32)
        p12 = [sb(f"p12_{i}", [128, 1024], BF16) for i in range(2)]
        scr = [sb(f"scr{i}", [128, 256], F32) for i in range(4)]
        ep_t2 = scr[0][:, 0:128]
        ep_diff = scr[1][:, 0:128]
        ep_sq = scr[2][:, 0:128]
        ep_on = scr[3][:, 0:128].bitcast(BF16)[:, 0:128]
        gl_sq, gl_v2, gl_v3, gl_t = (s[:] for s in scr)
        gl_up = scr[2][:]
        att_st = [sb(f"att_st{i}", [128, 128], BF16) for i in range(2)]

        ps_ab = psal("ps_ab", [128, 1024], F32)
        ps_cd = psal("ps_cd", [128, 1024], F32)
        ps_big = psal("ps_big", [128, 2048], F32)

        class _PV:
            def __init__(self, t, lo):
                self.t, self.lo = t, lo
            def __getitem__(self, key):
                assert isinstance(key, tuple) or key == slice(None)
                if not isinstance(key, tuple):
                    return self.t[:, self.lo:self.lo + 512]
                p, f = key
                assert p == slice(None)
                return self.t[:, self.lo + (f.start or 0):self.lo + f.stop]
        ps_a = _PV(ps_ab, 0)
        ps_b = _PV(ps_ab, 512)
        ps_c = _PV(ps_cd, 0)
        ps_d = _PV(ps_cd, 512)
        ps4 = [ps_a, ps_b, ps_c, ps_d]

        p = Prog(nc, ctx)
        V = SerialEngine(p, "vector", "sv")
        S = SerialEngine(p, "scalar", "ss", serial=False)
        G = SerialEngine(p, "gpsimd", "sg", serial=False)
        TE = TensorEng(p)
        Dsy = DmaQueue(p, "sync")
        Dsc = DmaQueue(p, "scalar")
        Dgp = DmaQueue(p, "gpsimd")

        m_ini = Dsy.dma_group("ld_ini", [
            (cst[:], consts.ap()),
            (id_b[:], ident.ap()),
            (lamv_sb, lamv.ap()),
        ])
        m_wk = Dgp.dma("ld_w0", wk_sb, wqkv_k.ap())
        ds_sc_sb = v8(AR1, 32768, 16384, BF16, 8192)
        m_dssc = Dgp.dma("ld_dssc", ds_sc_sb, ds_sc.ap())

        S.op("copy", lam_sb[:, 1:2], cst[:, 0:1], waits=[m_ini])

        V.op("tensor_tensor", lamx[:, 0:128], lamv_sb[:, 0:128],
             lamv_sb[:, 128:256], ALU.mult, waits=[m_ini])
        V.op("reduce_sum", lam_sb[:, 2:3], lamx[:, 0:64], axis=AX.X)
        m_d12 = V.op("reduce_sum", lam_sb[:, 3:4], lamx[:, 64:128], axis=AX.X)
        S.op("activation", lam_sb[:, 4:5], lam_sb[:, 2:3], AF.Exp,
             waits=[m_d12])
        m_e2 = S.op("activation", lam_sb[:, 5:6], lam_sb[:, 3:4], AF.Exp)
        V.op("tensor_tensor", lam_sb[:, 6:7], lam_sb[:, 4:5],
             lam_sb[:, 5:6], ALU.subtract, waits=[m_e2])
        m_lam = V.op("tensor_scalar", lam_sb[:, 0:1], lam_sb[:, 6:7],
                     LAMBDA_INIT, None, ALU.add)

        sp_xn_mark = [None, None]
        ln_v_mark = [None, None]
        for i in range(40):
            par = i % 2
            if i < 32:
                src_ = x_kv.ap()[128 * i:128 * (i + 1), :]
                dst = xn_kv_d.ap()[128 * i:128 * (i + 1), :]
            else:
                j = i - 32
                src_ = x_q.ap()[128 * j:128 * (j + 1), :]
                dst = xn_q_d.ap()[128 * j:128 * (j + 1), :]
            m_ld = Dsy.dma(f"ld_x{par}", xt[par], src_,
                           waits=[ln_v_mark[par], m_lam if i == 0 else None])
            V.op("reduce_sum", lnst[:, 0:1], xt[par], axis=AX.X,
                 waits=[m_ld])
            m_sq = S.op("activation", lnsq, xt[par], AF.Square,
                        accum_out=lnst[:, 1:2], waits=[m_ld, V.mark()])
            V.op("tensor_scalar", lnst[:, 2:3], lnst[:, 0:1], 1.0 / D, None,
                 ALU.mult, waits=[m_sq])
            V.op("tensor_tensor", lnst[:, 3:4], lnst[:, 2:3], lnst[:, 2:3],
                 ALU.mult)
            V.op("tensor_scalar", lnst[:, 4:5], lnst[:, 1:2], 1.0 / D, EPS,
                 ALU.mult, ALU.add)
            m_var = V.op("tensor_tensor", lnst[:, 5:6], lnst[:, 4:5],
                         lnst[:, 3:4], ALU.subtract)
            S.op("activation", lnst[:, 6:7], lnst[:, 5:6], AF.Ln,
                 waits=[m_var])
            m_rstd = S.op("activation", lnst[:, 7:8], lnst[:, 6:7], AF.Exp,
                          scale=-0.5)
            m_xn = V.op("tensor_scalar", xnt[par], xt[par],
                        lnst[:, 2:3], lnst[:, 7:8], ALU.subtract, ALU.mult,
                        waits=[m_rstd, sp_xn_mark[par]])
            ln_v_mark[par] = m_xn
            sp_xn_mark[par] = Dsc.dma(f"sp_xn{par}", dst, xnt[par],
                                      waits=[m_xn])
        m_ln_done = [sp_xn_mark[0], sp_xn_mark[1]]
        m_ln_v = V.mark()

        adl = [dict(c4=v8(AR1, 49152, 4096, I16, 2048),
                    s15=v8(AR1, 53248, 4096, I16, 2048),
                    xb=v8(AR1, 57344, 4096, I16, 2048),
                    xbf=v8(AR1, 57344, 4096, BF16, 2048),
                    r=v8(AR1, 61440, 4096, BF16, 2048),
                    pk=v8(AR3, 0, 1024, U8, 1024),
                    sc=v8(AR3, 1024, 2048, BF16, 1024)),
               dict(c4=v8(AR3, 3072, 4096, I16, 2048),
                    s15=v8(AR3, 7168, 4096, I16, 2048),
                    xb=v8(AR3, 11264, 4096, I16, 2048),
                    xbf=v8(AR3, 11264, 4096, BF16, 2048),
                    r=v8(AR3, 15360, 4096, BF16, 2048),
                    pk=v8(AR3, 19456, 1024, U8, 1024),
                    sc=v8(AR3, 20480, 2048, BF16, 1024))]
        ad_pm = [dict(), dict()]
        wdsp = {}
        ad_state = {"i1": 0, "i2": 0}

        def adq_s1(u):
            Jp, cc = u // 2, u % 2
            L = u % 2
            t = adl[L]
            pm = ad_pm[L]
            b, j = Jp // 2, Jp % 2
            m_pk = Dgp.dma(f"ld_pkD{L}", t["pk"],
                           down_pk3.ap()[b, :, 1024 * (2 * j + cc):
                                         1024 * (2 * j + cc + 1)],
                           waits=[pm.get("chi"), pm.get("clo")])
            p0 = (8 * Jp) % 128
            c0 = 2048 * (Jp // 16) + 1024 * cc
            src_sc = (ds_sc_sb[p0:p0 + 8, c0:c0 + 1024]
                      .unsqueeze(1).broadcast_to([8, 16, 1024]))
            m_sc = Dgp.dma(f"ld_scD{L}", t["sc"], src_sc,
                           waits=[pm.get("mul1"),
                                  m_dssc if u < 2 else None])
            m_chi = S.op("activation", t["c4"][:, 1024:2048], t["pk"],
                         AF.Identity, scale=0.0625, bias=-0.499999,
                         waits=[m_pk, pm.get("xb")])
            m_clo = V.op("scalar_tensor_tensor", t["c4"][:, 0:1024],
                         t["c4"][:, 1024:2048], -16.0, t["pk"],
                         ALU.mult, ALU.add, noserial=True,
                         waits=[m_chi, pm.get("xb")])
            m_s15 = V.op("tensor_scalar", t["s15"], t["c4"], 3, 15,
                         ALU.logical_shift_right, ALU.logical_shift_left,
                         noserial=True, waits=[m_clo, pm.get("orr")])
            m_and = V.op("tensor_single_scalar", t["c4"], t["c4"], 7,
                         ALU.bitwise_and, noserial=True, waits=[m_s15])
            m_xb = V.op("tensor_scalar", t["xb"], t["c4"], 64, 16128,
                        ALU.mult, ALU.add, noserial=True,
                        waits=[m_and] + _aslist(pm.get("sp")))
            pm.update(chi=m_chi, clo=m_clo, s15=m_s15, xb=m_xb, sc=m_sc)

        def adq_s2(u):
            Jp, cc = u // 2, u % 2
            L = u % 2
            t = adl[L]
            pm = ad_pm[L]
            b, j = Jp // 2, Jp % 2
            m_r = S.op("activation", t["r"], t["xb"], AF.Relu, bias=63.5,
                       scale=-1.0 / 256.0, waits=[pm["xb"], pm.get("val")])
            m_val = V.op("tensor_tensor", t["xbf"], t["xbf"], t["r"],
                         ALU.subtract, noserial=True, waits=[m_r])
            m_or = V.op("tensor_tensor", t["xb"], t["xb"], t["s15"],
                        ALU.add, noserial=True, waits=[m_val])
            m_m0 = V.op("tensor_tensor", t["xbf"][:, 0:1024],
                        t["xbf"][:, 0:1024], t["sc"], ALU.mult,
                        noserial=True, waits=[m_or, pm["sc"]])
            m_m1 = V.op("tensor_tensor", t["xbf"][:, 1024:2048],
                        t["xbf"][:, 1024:2048], t["sc"], ALU.mult,
                        noserial=True, waits=[m_m0])
            o0 = 4096 * cc + 2048 * j
            m_spa = Dsc.dma(f"sp_wd{L}a",
                            wdT_dram.ap()[b, :, o0:o0 + 1024],
                            t["xbf"][:, 0:1024], waits=[m_m0])
            m_spb = Dsc.dma(f"sp_wd{L}b",
                            wdT_dram.ap()[b, :, o0 + 1024:o0 + 2048],
                            t["xbf"][:, 1024:2048], waits=[m_m1])
            pm.update(r=m_r, val=m_val, orr=m_or, mul1=m_m1,
                      sp=[m_spa, m_spb])
            wdsp.setdefault((b, cc), []).extend([m_spa, m_spb])

        def emit_adq():
            if ad_state["i1"] < 128:
                adq_s1(ad_state["i1"])
                ad_state["i1"] += 1
            lim = ad_state["i1"] - (1 if ad_state["i1"] < 128 else 0)
            if ad_state["i2"] < lim:
                adq_s2(ad_state["i2"])
                ad_state["i2"] += 1

        qkv_ps = [ps_a, ps_c]
        ps_war = [None, None]
        tr_bank_war = [None, None]
        xnT_war = [None, None]
        kq_war = [None, None]
        vtr_war = [None, None]
        vexev_war = [None, None]
        k_sp = v_sp = q_sp = None
        m_vex_init = None

        def qkv_pass(tiles, xn_src, m_w, mode):
            nonlocal k_sp, v_sp, q_sp, m_vex_init
            for jj, j in enumerate(tiles):
                par = jj % 2
                t0 = (j - 8) * 512 if mode == "q" else j * 512
                wlist = list(m_ln_done) + _aslist(xnT_war[par])
                for c0 in range(16):
                    Dsy.dma(f"ld_xnT{par}",
                            xnT_sb[par][:, 512 * c0:512 * (c0 + 1)],
                            xn_src.ap()[t0:t0 + 512, 128 * c0:128 * (c0 + 1)],
                            waits=wlist if c0 == 0 else (), transpose=True)
                m_xnT = p.mark(f"ld_xnT{par}")
                if mode == "q":
                    for c0 in range(16):
                        Dsc.dma("sp_xnqT",
                                xnqT_d.ap()[128 * c0:128 * (c0 + 1), t0:t0 + 512],
                                xnT_sb[par][:, 512 * c0:512 * (c0 + 1)],
                                waits=[m_xnT] if c0 == 0 else ())
                TE.wait(m_xnT)
                TE.wait(m_w)
                for cc in range(16):
                    ppi = cc % 2
                    pp = qkv_ps[ppi]
                    TE.wait(ps_war[ppi])
                    m_mm = None
                    for hc in range(16):
                        kw = dict(start=(hc == 0), stop=(hc == 15))
                        lhsT = wk_sb[:, 2048 * hc + 128 * cc:
                                     2048 * hc + 128 * (cc + 1)]
                        rhs = xnT_sb[par][:, 512 * hc:512 * (hc + 1)]
                        if hc == 15:
                            m_mm = TE.mile("matmul", pp[:], lhsT, rhs, **kw)
                        else:
                            TE.mm(pp[:], lhsT, rhs, **kw)
                    bufp = cc % 2
                    if mode in ("k", "q"):
                        m_ev = S.op("copy", qkv_ev[bufp], pp[:],
                                    waits=[m_mm, kq_war[bufp],
                                           m_ln_v if (jj == 0 and cc < 2) else None])
                        ps_war[ppi] = m_ev
                        if mode == "q":
                            m_sp = Dsc.dma("sp_q", q12T_d.ap()[cc, :, t0:t0 + 512],
                                           qkv_ev[bufp], waits=[m_ev])
                            q_sp = m_sp
                        else:
                            m_sp = Dsc.dma("sp_k", k12T_d.ap()[cc, :, t0:t0 + 512],
                                           qkv_ev[bufp], waits=[m_ev])
                            k_sp = m_sp
                        kq_war[bufp] = m_sp
                    else:
                        h = cc
                        m_ev = S.op("copy", vtr_ev[bufp], pp[:],
                                    waits=[m_mm, vtr_war[bufp]])
                        ps_war[ppi] = m_ev
                        cp_marks = [None] * 4
                        for kk in range(4):
                            bank = [ps_b, ps_d][kk % 2]
                            dstps = bank[:, 64 * (kk // 2):
                                         64 * (kk // 2 + 1)].bitcast(BF16)
                            if kk == 0:
                                TE.wait(m_ev)
                                TE.wait(m_ini)
                            TE.wait(cp_marks[kk - 2] if kk >= 2
                                    else tr_bank_war[kk % 2])
                            m_tr = TE.mile("transpose", dstps,
                                           vtr_ev[bufp][:, 128 * kk:128 * (kk + 1)],
                                           id_b[:])
                            cp_marks[kk] = S.op(
                                "copy", vex_ev[bufp][:, 132 * kk:132 * kk + 128],
                                dstps,
                                waits=[m_tr,
                                       vexev_war[bufp] if kk == 0 else None,
                                       m_vex_init if kk == 0 else None])
                        tr_bank_war[0] = cp_marks[2]
                        tr_bank_war[1] = cp_marks[3]
                        vtr_war[bufp] = cp_marks[3]
                        m_sp = Dsc.dma(
                            "sp_v", vext_d.ap()[h, :, 4 * j:4 * (j + 1), :],
                            vex_ev[bufp].rearrange("p (a f) -> p a f", a=4),
                            waits=[cp_marks[3]])
                        vexev_war[bufp] = m_sp
                        v_sp = m_sp
                    if cc % 4 == 1:
                        emit_adq()
                xnT_war[par] = TE.fence()

        m_vex_init = None
        for i2 in range(2):
            V.op("memset", vex_ev[i2], 0.0, waits=[m_ln_v])
            for kk in range(4):
                m_vex_init = V.op("memset",
                                  vex_ev[i2][:, 132 * kk + 128:132 * kk + 129],
                                  1.0)
        qkv_pass(list(range(8)), xn_kv_d, m_wk, "k")
        m_wv = Dgp.dma("ld_w1", wk_sb, wqkv_v.ap(), waits=TE.fence())
        qkv_pass(list(range(8)), xn_kv_d, m_wv, "v")
        m_wq2 = Dgp.dma("ld_w2", wk_sb, wqkv_q.ap(), waits=TE.fence())
        qkv_pass([8, 9], xn_q_d, m_wq2, "q")
        m_wp = Dgp.dma("ld_w3", wk_sb, wproj.ap(), waits=TE.fence())

        def off(mat, sub):
            return (sub // 2 + 2 * mat) * 512 + (sub % 2) * 129

        m_qkv_done = [p.mark("sp_k"), p.mark("sp_v"), p.mark("sp_q"),
                      p.mark("sp_xnqT")]
        msk_sb = v8(AR1, 0, 32768, BF16, 32 * 512)
        m_msk = Dsy.dma("ld_msk", msk_sb, masks.ap(), waits=m_qkv_done)
        m_msk_last = None

        hd_war = [None, None]
        exp_war = {}
        av_war = {}
        ep_copies = []
        att_sp = [None, None]
        first_att = True
        OBASE = [0, 512, 1024, 1536]

        for h in range(NH):
            hb = h % 2
            m_k = Dsy.dma(f"ld_k{hb}", k12[hb], k12T_d.ap()[h],
                          waits=[k_sp, v_sp, q_sp] + _aslist(hd_war[hb]))
            m_v = Dsy.dma(f"ld_v{hb}",
                          vex[hb].rearrange("p (a f) -> p a f", a=32),
                          vext_d.ap()[h])
            m_q = Dsy.dma(f"ld_q{hb}", q12[hb], q12T_d.ap()[h])
            for jt in range(4):
                prev_ep = list(ep_copies)
                ep_copies = []
                msk_marks = {}
                last = None

                def emit_scores(kc):
                    sp = kc % 2
                    pst = [ps_ab, ps_cd][sp]
                    pbuf = p12[(kc % 4) // 2][:, 512 * sp:512 * (sp + 1)]
                    if kc == 0:
                        TE.wait(m_k)
                        TE.wait(m_q)
                        TE.wait(m_v)
                        for mk in prev_ep:
                            TE.wait(mk)
                    if first_att and kc < 2:
                        TE.wait(ps_war[sp])
                        TE.wait(tr_bank_war[0])
                        TE.wait(tr_bank_war[1])
                    TE.wait(exp_war.get(sp))
                    TE.mm(pst[:, 0:256], k12[hb][0:64, 128 * kc:128 * (kc + 1)],
                          q12[hb][0:64, 256 * jt:256 * (jt + 1)],
                          start=True, stop=True)
                    m_s2 = TE.mile("matmul", pst[:, 512:768],
                                   k12[hb][64:128, 128 * kc:128 * (kc + 1)],
                                   q12[hb][64:128, 256 * jt:256 * (jt + 1)],
                                   start=True, stop=True)
                    sview = pst[:].rearrange("p (b c) -> p b c", b=2)[:, :, 0:256]
                    m_exp = S.op("activation",
                                 pbuf.rearrange("p (b c) -> p b c", b=2),
                                 sview, AF.Exp, bias=-12.0, scale=0.125,
                                 waits=[m_s2, av_war.get(kc % 4)])
                    exp_war[sp] = m_exp
                    base = PC[jt] - 8
                    if kc >= base:
                        mslice = msk_sb[:, (jt * 8 + kc - base) * 512:
                                        (jt * 8 + kc - base + 1) * 512]
                        m2 = V.op("tensor_tensor", pbuf, pbuf, mslice,
                                  ALU.mult, noserial=True,
                                  waits=[m_exp, m_msk, av_war.get(kc % 4)])
                        msk_marks[kc] = m2
                    else:
                        msk_marks[kc] = m_exp

                def emit_av(kc):
                    nonlocal last
                    sp = kc % 2
                    pbuf = p12[(kc % 4) // 2][:, 512 * sp:512 * (sp + 1)]
                    TE.wait(msk_marks[kc])
                    for mat in range(2):
                        for sub in range(2):
                            o = OBASE[2 * mat + sub]
                            kw = dict(start=(kc == 0), stop=(kc == PC[jt] - 1))
                            lhsT = pbuf[:, 256 * mat + 128 * sub:
                                        256 * mat + 128 * (sub + 1)]
                            rhs = vex[hb][:, 132 * kc:132 * kc + 129]
                            if (mat, sub) == (1, 1):
                                last = TE.mile("matmul", ps_big[:, o:o + 129],
                                               lhsT, rhs, **kw)
                            else:
                                TE.mm(ps_big[:, o:o + 129], lhsT, rhs, **kw)
                    av_war[kc % 4] = last

                for kc in range(PC[jt] + 2):
                    if kc < PC[jt]:
                        emit_scores(kc)
                    if kc >= 2:
                        emit_av(kc - 2)
                m_msk_last = msk_marks[PC[jt] - 1]
                first_att = False
                m_av_done = last
                eps = []
                for sub in range(2):
                    c0 = 8 * sub
                    o1 = OBASE[sub]
                    o2 = OBASE[2 + sub]
                    t2v = scr[0][:, 128 * sub:128 * (sub + 1)]
                    dfv = scr[1][:, 128 * sub:128 * (sub + 1)]
                    sqv = scr[2][:, 128 * sub:128 * (sub + 1)]
                    onv = (scr[3][:, 128 * sub:128 * (sub + 1)]
                           .bitcast(BF16)[:, 0:128])
                    V.op("reciprocal", ep_st[:, c0:c0 + 1],
                         ps_big[:, o1 + 128:o1 + 129],
                         waits=[m_av_done, m_lam] if sub == 0 else None)
                    V.op("reciprocal", ep_st[:, c0 + 1:c0 + 2],
                         ps_big[:, o2 + 128:o2 + 129])
                    V.op("tensor_tensor", ep_st[:, c0 + 2:c0 + 3],
                         ep_st[:, c0 + 1:c0 + 2],
                         lam_sb[:, 0:1], ALU.mult)
                    V.op("tensor_scalar", t2v,
                         ps_big[:, o2:o2 + 128],
                         ep_st[:, c0 + 2:c0 + 3], None, ALU.mult)
                    m_diff = V.op("scalar_tensor_tensor", dfv,
                                  ps_big[:, o1:o1 + 128],
                                  ep_st[:, c0:c0 + 1], t2v, ALU.mult,
                                  ALU.subtract)
                    S.op("activation", sqv, dfv, AF.Square,
                         accum_out=ep_st[:, c0 + 3:c0 + 4], waits=[m_diff])
                    S.op("activation", ep_st[:, c0 + 4:c0 + 5],
                         ep_st[:, c0 + 3:c0 + 4], AF.Ln,
                         bias=EPS, scale=1.0 / VD)
                    m_rs = S.op("activation", ep_st[:, c0 + 5:c0 + 6],
                                ep_st[:, c0 + 4:c0 + 5],
                                AF.Exp, bias=math.log(ONE_M_LI), scale=-0.5)
                    eps.append((m_rs, dfv, onv, c0))
                for sub in range(2):
                    m_rs, dfv, onv, c0 = eps[sub]
                    stp = sub % 2
                    m_on = V.op("tensor_scalar", onv, dfv,
                                ep_st[:, c0 + 5:c0 + 6], None, ALU.mult,
                                waits=[m_rs])
                    bank = [ps_b, ps_d][stp]
                    dstps = bank[:, 384:448].bitcast(BF16)
                    TE.wait(m_on)
                    TE.wait(tr_bank_war[stp])
                    m_tr = TE.mile("transpose", dstps, onv, id_b[:])
                    m_cp = V.op("tensor_copy", att_st[stp][:], dstps,
                                waits=[m_tr, att_sp[stp]])
                    qs = 2 * jt + sub
                    att_sp[stp] = Dsc.dma("sp_att",
                                          attnT_d.ap()[qs, :, h, :],
                                          att_st[stp][:], waits=[m_cp])
                    ep_copies.append(m_cp)
                    tr_bank_war[stp] = m_cp
                emit_adq()
            hd_war[hb] = TE.fence()
        while ad_state["i2"] < 128:
            emit_adq()

        gu_sc_sb = v8(AR2, 16384, 32768, BF16, NF)
        m_gusc = Dgp.dma("ld_gusc", gu_sc_sb, gu_sc.ap(),
                         waits=_aslist([hd_war[0], hd_war[1]]))

        m_att_all = p.mark("sp_att")
        aq_war = [None, None]
        sev_war = None
        m_sp_s = None
        prev_guard = list(ep_copies)
        hd_fence = TE.fence()
        for qs in range(8):
            bufp = qs % 2
            m_aq = Dsy.dma(f"ld_aq{bufp}",
                           att_qs[bufp].rearrange("p (a f) -> p a f", a=16),
                           attnT_d.ap()[qs],
                           waits=[m_att_all] + (hd_fence if qs < 2 else [])
                                 + _aslist(aq_war[bufp]))
            TE.wait(m_aq)
            if qs == 0:
                TE.wait(m_wp)
            for mk in prev_guard:
                TE.wait(mk)
            m_pj = None
            for h in range(NH):
                for c4 in range(4):
                    kw = dict(start=(h == 0), stop=(h == 15))
                    lhsT = att_qs[bufp][:, 128 * h:128 * (h + 1)]
                    rhs = wk_sb[:, 2048 * h + 512 * c4:2048 * h + 512 * (c4 + 1)]
                    if (h, c4) == (15, 3):
                        m_pj = TE.mile("matmul",
                                       ps_big[:, 512 * c4:512 * (c4 + 1)],
                                       lhsT, rhs, **kw)
                    else:
                        TE.mm(ps_big[:, 512 * c4:512 * (c4 + 1)], lhsT, rhs,
                              **kw)
            aq_war[bufp] = TE.fence()
            m_ev = V.op("tensor_copy", s_ev[0], ps_big[:],
                        waits=[m_pj, sev_war])
            m_sp_s = Dsc.dma("sp_s", s_d.ap()[128 * qs:128 * (qs + 1), :],
                             s_ev[0], waits=[m_ev])
            sev_war = m_sp_s
            prev_guard = [m_ev]
        m_proj_ev = prev_guard[0]

        m_p5_te = TE.fence()
        mb_v = V.mark()
        mb_s = S.mark()
        m_spwd = [p.mark("sp_wd0a"), p.mark("sp_wd0b"),
                  p.mark("sp_wd1a"), p.mark("sp_wd1b")]
        m_scD = [p.mark("ld_scD0"), p.mark("ld_scD1")]

        xnqA = v8(AR1, 0, 32768, BF16, 16 * NQ)
        glA = [v8(AR1, 32768 + 4096 * i, 4096, F32, NQ) for i in range(4)]
        c4A = [v8(AR3, 16384, 2048, I16, 1024),
               v8(AR1, 49152, 2048, I16, 1024)]
        s15A = [v8(AR3, 18432, 2048, I16, 1024),
                v8(AR1, 51200, 2048, I16, 1024)]
        rA = [v8(AR3, 20480, 2048, BF16, 1024),
              v8(AR1, 59392, 2048, BF16, 1024)]
        scb2A = [v8(AR1, 53248, 1024, BF16, 512),
                 v8(AR1, 54272, 1024, BF16, 512)]
        ievA = [v8(AR1, 55296, 2048, BF16, NQ), v8(AR1, 57344, 2048, BF16, NQ)]
        pkA = [[v8(AR3, 0, 4096, U8, 4096), v8(AR3, 4096, 4096, U8, 4096)],
               [v8(AR3, 8192, 4096, U8, 4096), v8(AR3, 12288, 4096, U8, 4096)]]
        gubuf = [[v8(AR0, 0, 16384, BF16, 8192),
                  v8(AR0, 16384, 16384, BF16, 8192)],
                 [v8(AR0, 32768, 16384, BF16, 8192),
                  v8(AR0, 49152, 16384, BF16, 8192)]]
        gubuf_i = [[v8(AR0, 0, 16384, I16, 8192),
                    v8(AR0, 16384, 16384, I16, 8192)],
                   [v8(AR0, 32768, 16384, I16, 8192),
                    v8(AR0, 49152, 16384, I16, 8192)]]

        for g in range(16):
            J2, par2 = divmod(g, 2)
            src_ = xnqT_d.ap()[256 * J2 + par2:256 * J2 + 256:2, 0:NQ]
            w0 = [p.mark("sp_xnqT"), m_msk_last] if g == 0 else []
            Dsy.dma("ld_xnq", xnqA[:, NQ * g:NQ * (g + 1)], src_, waits=w0)
        m_xnq = p.mark("ld_xnq")

        pmA = [dict(xb=[mb_v, mb_s], orr=mb_v, mul1=mb_v, val=m_spwd),
               dict(xb=[mb_v, mb_s], orr=mb_v, mul1=mb_v, val=m_spwd)]
        guready = {}
        fenceA = {}
        pkA_marks = {}
        ispill = {}
        ievA_war = [[mb_v] + m_spwd, [mb_v] + m_spwd]
        psA_war = {0: [mb_v, mb_s], 1: [mb_v, mb_s]}
        sqA_war = [[mb_v] + m_scD]

        def ffnA_s1(s, u):
            mat, J = u % 2, u // 2
            pm = pmA[mat]
            dbuf = s % 2
            pk = pkA[mat][dbuf][:, 512 * J:512 * (J + 1)]
            c4, s15, scb2 = c4A[mat], s15A[mat], scb2A[mat]
            src_sc = (gu_sc_sb[64 * mat + 8 * J:64 * mat + 8 * J + 8,
                               512 * s:512 * (s + 1)]
                      .unsqueeze(1).broadcast_to([8, 16, 512]))
            m_sc = Dsc.dma(f"ld_scA{mat}", scb2, src_sc,
                           waits=[pm.get("mul1"),
                                  m_gusc if s == 0 and J == 0 else None])
            m_chi = S.op("activation", c4[:, 512:1024], pk, AF.Identity,
                         scale=0.0625, bias=-0.499999,
                         waits=[pm.get("xb"), pkA_marks[(mat, dbuf)]])
            m_clo = V.op("scalar_tensor_tensor", c4[:, 0:512], c4[:, 512:1024],
                         -16.0, pk, ALU.mult, ALU.add, noserial=True,
                         waits=[m_chi, pm.get("xb")])
            m_s15 = V.op("tensor_scalar", s15, c4, 3, 15,
                         ALU.logical_shift_right, ALU.logical_shift_left,
                         noserial=True, waits=[m_clo, pm.get("orr")])
            m_and = V.op("tensor_single_scalar", c4, c4, 7, ALU.bitwise_and,
                         noserial=True, waits=[m_s15])
            xb_i = gubuf_i[mat][dbuf][:, 1024 * J:1024 * (J + 1)]
            wl = [m_and] + _aslist(fenceA.get(s - 2, m_p5_te))
            m_xb = V.op("tensor_scalar", xb_i, c4, 64, 16128,
                        ALU.mult, ALU.add, noserial=True, waits=wl)
            pm.update(sc=m_sc, chi=m_chi, clo=m_clo, s15=m_s15, xb=m_xb)

        def ffnA_s2(s, u):
            mat, J = u % 2, u // 2
            pm = pmA[mat]
            dbuf = s % 2
            r, s15, scb2 = rA[mat], s15A[mat], scb2A[mat]
            xb_i = gubuf_i[mat][dbuf][:, 1024 * J:1024 * (J + 1)]
            xb_bf = gubuf[mat][dbuf][:, 1024 * J:1024 * (J + 1)]
            m_r = S.op("activation", r, xb_i, AF.Relu, bias=63.5,
                       scale=-1.0 / 256.0, waits=[pm["xb"], pm.get("val")])
            m_val = V.op("tensor_tensor", xb_bf, xb_bf, r, ALU.subtract,
                         noserial=True, waits=[m_r])
            m_or = V.op("tensor_tensor", xb_i, xb_i, s15, ALU.add,
                        noserial=True, waits=[m_val])
            m_m0 = V.op("tensor_tensor", xb_bf[:, 0:512], xb_bf[:, 0:512],
                        scb2, ALU.mult, noserial=True,
                        waits=[m_or, pm["sc"]])
            m_m1 = V.op("tensor_tensor", xb_bf[:, 512:1024],
                        xb_bf[:, 512:1024], scb2, ALU.mult, noserial=True,
                        waits=[m_m0])
            pm.update(r=m_r, val=m_val, orr=m_or, mul1=m_m1)
            guready.setdefault(s, []).append(m_m1)

        unitsA = []
        dqA = {"i1": 0, "i2": 0}

        def emitA_unit():
            if dqA["i1"] < len(unitsA):
                ffnA_s1(*unitsA[dqA["i1"]])
                dqA["i1"] += 1
            lim = dqA["i1"] - (1 if dqA["i1"] < len(unitsA) else 0)
            if dqA["i2"] < lim:
                ffnA_s2(*unitsA[dqA["i2"]])
                dqA["i2"] += 1

        def emitA_chunk(s, ch):
            dbuf = s % 2
            par = ch % 2
            C = 4 * s + ch
            if ch == 0:
                for mk in guready[s]:
                    TE.wait(mk)
                TE.wait(m_xnq)
            for mk in _aslist(psA_war[par]):
                TE.wait(mk)
            m_g = m_u2 = None
            for mat in range(2):
                for th in range(2):
                    for g in range(16):
                        kw = dict(start=(g == 0), stop=(g == 15))
                        off = 1024 * (g // 2) + 512 * (g % 2) + 128 * ch
                        lhsT = gubuf[mat][dbuf][:, off:off + 128]
                        rhs = xnqA[:, NQ * g + 512 * th:NQ * g + 512 * (th + 1)]
                        if par == 0:
                            dst = [ps_ab, ps_cd][mat][:, 512 * th:512 * (th + 1)]
                        else:
                            dst = ps_big[:, 1024 * mat + 512 * th:
                                         1024 * mat + 512 * (th + 1)]
                        if (th, g) == (1, 15):
                            mk = TE.mile("matmul", dst, lhsT, rhs, **kw)
                            if mat == 0:
                                m_g = mk
                            else:
                                m_u2 = mk
                        else:
                            TE.mm(dst, lhsT, rhs, **kw)
            pgf = ps_ab[:, 0:1024] if par == 0 else ps_big[:, 0:1024]
            puf = ps_cd[:, 0:1024] if par == 0 else ps_big[:, 1024:2048]
            m_sq2 = S.op("activation", glA[0], pgf, AF.Square, scale=SQ_GC3,
                         waits=[m_g] + _aslist(sqA_war[0]))
            m_v3 = V.op("scalar_tensor_tensor", glA[2], glA[0], GC1, pgf,
                        ALU.add, ALU.mult, waits=[m_sq2])
            m_t = S.op("activation", glA[3], glA[2], AF.Tanh, waits=[m_v3])
            m_up = S.op("activation", glA[0], puf, AF.Identity, scale=0.5,
                        waits=[m_u2, m_v3])
            m_pm = V.op("tensor_tensor", glA[1], pgf, glA[0], ALU.mult,
                        waits=[m_up])
            m_int = V.op("scalar_tensor_tensor", ievA[ch % 2], glA[3], 1.0,
                         glA[1], ALU.add, ALU.mult,
                         waits=[m_t, ievA_war[ch % 2]])
            m_isp = Dsy.dma(f"sp_int{ch % 2}", inter_d.ap()[C], ievA[ch % 2],
                            waits=[m_int])
            ispill[C] = m_isp
            ievA_war[ch % 2] = m_isp
            psA_war[par] = [m_pm, m_up]
            sqA_war[0] = m_pm

        seedA = [mb_v, mb_s] + m_spwd
        pkA_war = [[seedA, seedA], [seedA, seedA]]
        for it in range(33):
            s = it
            prev = it - 1
            if s < 32:
                dbuf = s % 2
                for mat in range(2):
                    pkA_marks[(mat, dbuf)] = Dsy.dma(
                        f"ld_pkA{mat}{dbuf}", pkA[mat][dbuf],
                        [gate_pk3, up_pk3][mat].ap()[s],
                        waits=_aslist(pkA_war[mat][dbuf]))
                unitsA.extend((s, u) for u in range(16))
            for step in range(4):
                for _ in range(4):
                    emitA_unit()
                if prev >= 0:
                    emitA_chunk(prev, step)
            if s < 32:
                for mat in range(2):
                    pkA_war[mat][s % 2] = [pmA[mat]["chi"], pmA[mat]["clo"]]
            if prev >= 0:
                fenceA[prev] = TE.fence()
        while dqA["i2"] < len(unitsA):
            emitA_unit()
        mA_v = V.mark()
        mA_s = S.mark()
        m_scA_done = [p.mark("ld_scA0"), p.mark("ld_scA1")]

        wdbB = [v8(AR1, 0, 8192, BF16, 4096), v8(AR1, 8192, 8192, BF16, 4096)]
        ibufB = [v8(AR1, 16384, 4096, BF16, 2048),
                 v8(AR1, 20480, 4096, BF16, 2048)]
        finxB = [v8(AR1, 24576, 4096, F32, 1024),
                 v8(AR1, 28672, 4096, F32, 1024)]
        finsB = [v8(AR1, 32768, 4096, F32, 1024),
                 v8(AR1, 36864, 4096, F32, 1024)]
        finlB = [v8(AR1, 40960, 4096, F32, 1024),
                 v8(AR1, 45056, 4096, F32, 1024)]

        fenceB = {}
        m_arB = [mA_v, mA_s] + _aslist(fenceA[31]) + _aslist(fenceA.get(30))
        psq_war = [m_arB] * 4
        fin_war = [[None, None], [None, None], [None, None]]
        for pi in range(4):
            hh, tg = pi // 2, pi % 2
            m_blast = None
            for b in range(32):
                dbuf = b % 2
                bg = 32 * pi + b
                wl = wdsp[(b, hh)] + _aslist(fenceB.get(bg - 2))
                if pi == 0 and b < 2:
                    wl += m_arB
                m_wd = Dsy.dma(f"ld_wdB{dbuf}", wdbB[dbuf],
                               wdT_dram.ap()[b, :, 4096 * hh:4096 * (hh + 1)],
                               waits=wl)
                ibm = []
                for g in range(4):
                    wl = [ispill[4 * b + g]] if pi == 0 else []
                    if g == 0:
                        wl = wl + _aslist(fenceB.get(bg - 2))
                        if pi == 0 and b < 2:
                            wl = wl + [mA_v]
                    ibm.append(Dsc.dma(
                        f"ld_intB{dbuf}",
                        ibufB[dbuf][:, 512 * g:512 * (g + 1)],
                        inter_d.ap()[4 * b + g, :, 512 * tg:512 * (tg + 1)],
                        waits=wl))
                TE.wait(m_wd)
                for mk in ibm:
                    TE.wait(mk)
                if b == 0:
                    for mk in _aslist(psq_war):
                        TE.wait(mk)
                for tc4 in range(4):
                    for g in range(4):
                        kw = dict(start=(b == 0 and g == 0),
                                  stop=(b == 31 and g == 3))
                        lhsT = ibufB[dbuf][:, 512 * g + 128 * tc4:
                                           512 * g + 128 * (tc4 + 1)]
                        for hq in range(2):
                            rhs = wdbB[dbuf][:, 1024 * g + 512 * hq:
                                             1024 * g + 512 * (hq + 1)]
                            if tc4 < 2:
                                dst = [ps_ab, ps_cd][tc4][:, 512 * hq:
                                                          512 * (hq + 1)]
                            else:
                                dst = ps_big[:, 1024 * (tc4 - 2) + 512 * hq:
                                             1024 * (tc4 - 2) + 512 * (hq + 1)]
                            if (tc4, g, hq) == (3, 3, 1):
                                m_blast = TE.mile("matmul", dst, lhsT, rhs,
                                                  **kw)
                            else:
                                TE.mm(dst, lhsT, rhs, **kw)
                fenceB[bg] = TE.fence()
            nw = []
            for tc4 in range(4):
                tc = 4 * tg + tc4
                fp = tc4 % 2
                psq = [ps_ab[:, 0:1024], ps_cd[:, 0:1024],
                       ps_big[:, 0:1024], ps_big[:, 1024:2048]][tc4]
                wl = [fin_war[0][fp]]
                if pi == 0 and tc4 < 2:
                    wl += m_scA_done + _aslist(fenceA[31])
                m_fx = Dsy.dma(f"ld_fx{fp}", finxB[fp],
                               x_q.ap()[128 * tc:128 * (tc + 1),
                                        1024 * hh:1024 * (hh + 1)], waits=wl)
                m_fs = Dsy.dma(f"ld_fs{fp}", finsB[fp],
                               s_d.ap()[128 * tc:128 * (tc + 1),
                                        1024 * hh:1024 * (hh + 1)],
                               waits=[m_sp_s if (pi == 0 and tc4 == 0) else None,
                                      fin_war[1][fp]])
                wl = [m_blast, fin_war[2][fp]]
                if pi == 0 and tc4 < 2:
                    wl += [mA_v, mA_s] + m_scD
                m_sil = S.op("activation", finlB[fp], psq, AF.Silu, waits=wl)
                m_o1 = V.op("tensor_tensor", finlB[fp], finlB[fp], finsB[fp],
                            ALU.mult, waits=[m_sil, m_fs])
                m_o2 = V.op("tensor_tensor", finlB[fp], finlB[fp], finxB[fp],
                            ALU.add, waits=[m_fx])
                m_out = Dsc.dma("sp_out",
                                out.ap()[128 * tc:128 * (tc + 1),
                                         1024 * hh:1024 * (hh + 1)],
                                finlB[fp], waits=[m_o2])
                fin_war[0][fp] = m_o2
                fin_war[1][fp] = m_o1
                fin_war[2][fp] = [m_o1, m_out]
                nw.append(m_sil)
            psq_war = nw

        p.wait("sync", p.mark("sp_out"))

        with nc.Block() as block:
            @block.tensor
            def _(eng):
                for f in p.body["tensor"]:
                    f(eng)

            @block.vector
            def _(eng):
                for f in p.body["vector"]:
                    f(eng)

            @block.scalar
            def _(eng):
                for f in p.body["scalar"]:
                    f(eng)

            @block.gpsimd
            def _(eng):
                for f in p.body["gpsimd"]:
                    f(eng)

            @block.sync
            def _(eng):
                for f in p.body["sync"]:
                    f(eng)

        res = run_bass_kernel_spmd(nc, in_maps, core_ids=list(range(N_CORES)),
                                   trace=trace)
    return res


def kernel(**inputs):
    in_maps = _host_inputs(inputs)
    res = build_and_run(in_maps)
    full = np.empty((2, T, D), np.float32)
    for c in range(N_CORES):
        b, q = divmod(c, 4)
        oc = np.asarray(res.results[c]["out"])
        for jt, t in enumerate(TILES[q]):
            full[b, 256 * t:256 * (t + 1)] = oc[256 * jt:256 * (jt + 1)]
    return full


# version 16 (final)
# speedup vs baseline: 1.1040x; 1.1040x over previous
import math
import numpy as np
import concourse.bass as bass
import concourse.mybir as mybir
from concourse.bass_utils import run_bass_kernel_spmd

F32 = mybir.dt.float32
BF16 = mybir.dt.bfloat16
U8 = mybir.dt.uint8
I16 = mybir.dt.int16
AF = mybir.ActivationFunctionType
ALU = mybir.AluOpType
AX = mybir.AxisListType

D = 2048
T = 4096
NQ = 1024
NH = 16
VD = 128
NF = 16384
LAMBDA_INIT = 0.8 - 0.6 * math.exp(-0.3)
ONE_M_LI = 1.0 - LAMBDA_INIT
GC1 = 0.7978845608028654
GC3 = 0.7978845608028654 * 0.044715
EPS = 1e-5

SQ_GC3 = math.sqrt(GC3)
CONST_VALS = [0.0, 1.0, 0.125, -12.0, 0.5, -0.25, EPS, 1.0 / D, 1.0 / VD,
              -0.5, math.log(ONE_M_LI), GC1, GC3, LAMBDA_INIT, 0.0625,
              -0.499999, 63.5, -1.0 / 256.0, SQ_GC3]

N_CORES = 8
TILES = [[c4, 4 + c4, 11 - c4, 15 - c4] for c4 in range(4)]
PC = [8, 16, 24, 32]
MOFF = [0, 8, 24, 48]
NCH_TOT = 80



def _nperm():
    p = []
    for b in range(NF // 256):
        p.extend(range(256 * b, 256 * (b + 1), 2))
        p.extend(range(256 * b + 1, 256 * (b + 1), 2))
    return np.array(p)


def _pack_u8(a):
    a = np.ascontiguousarray(np.asarray(a, np.int32))
    return np.ascontiguousarray(a.view(np.uint8).reshape(*a.shape, 4)[..., 0])


def _chunkmajor(w):
    Dr, C = w.shape
    return np.ascontiguousarray(w.reshape(Dr // 128, 128, C).transpose(1, 0, 2)
                                .reshape(128, (Dr // 128) * C))


def _host_inputs(inputs):
    x = np.asarray(inputs["L0_attn_x"], np.float32).reshape(2, T, D)
    wqkv = np.asarray(inputs["L0_attn_Wqkv"], np.float32)
    wproj = np.asarray(inputs["L0_attn_Wproj"], np.float32)

    kcols, vcols, qcols = [], [], []
    for h in range(NH):
        kcols.extend(range(1024 + 64 * h, 1024 + 64 * h + 64))
        kcols.extend(range(3072 + 64 * h, 3072 + 64 * h + 64))
    for h in range(NH):
        vcols.extend(range(4096 + 128 * h, 4096 + 128 * h + 128))
    for h in range(NH):
        qcols.extend(range(64 * h, 64 * h + 64))
        qcols.extend(range(2048 + 64 * h, 2048 + 64 * h + 64))
    wqkv_k = _chunkmajor(np.ascontiguousarray(wqkv[:, kcols]))
    wqkv_v = _chunkmajor(np.ascontiguousarray(wqkv[:, vcols]))
    wqkv_q = _chunkmajor(np.ascontiguousarray(wqkv[:, qcols]))
    wproj_cm = _chunkmajor(np.ascontiguousarray(wproj))

    lamv = np.zeros((128, 256), np.float32)
    lamv[:, 0:64] = np.asarray(inputs["L0_attn_lambda_q1"], np.float32)
    lamv[:, 64:128] = np.asarray(inputs["L0_attn_lambda_q2"], np.float32)
    lamv[:, 128:192] = np.asarray(inputs["L0_attn_lambda_k1"], np.float32)
    lamv[:, 192:256] = np.asarray(inputs["L0_attn_lambda_k2"], np.float32)

    import ml_dtypes
    consts = np.zeros((128, 20), np.float32)
    consts[:, :len(CONST_VALS)] = CONST_VALS
    ident = np.eye(128, dtype=ml_dtypes.bfloat16)

    perm = _nperm()

    def _gu_blocks(pk_t):
        a = pk_t.reshape(8, 128, 32, 512)
        return np.ascontiguousarray(a.transpose(2, 1, 0, 3).reshape(32, 128, 4096))

    gate_pk3 = _gu_blocks(_pack_u8(inputs["L0_ffn_gate_W_packed"])[perm, :].T)
    up_pk3 = _gu_blocks(_pack_u8(inputs["L0_ffn_up_W_packed"])[perm, :].T)
    d8 = _pack_u8(inputs["L0_ffn_down_W_packed"]).T
    down_pk3 = np.ascontiguousarray(
        d8.reshape(32, 2, 128, 2, 1024).transpose(0, 2, 1, 3, 4)
        .reshape(32, 128, 4096))
    gsc_t = np.asarray(inputs["L0_ffn_gate_scales"], np.float32)[perm, :].T
    usc_t = np.asarray(inputs["L0_ffn_up_scales"], np.float32)[perm, :].T
    gu_sc = np.ascontiguousarray(np.concatenate([gsc_t, usc_t], 0))
    dsc_t = np.asarray(inputs["L0_ffn_down_scales"], np.float32).T
    ds_sc = np.ascontiguousarray(
        dsc_t.reshape(4, 128, 2048).transpose(1, 0, 2).reshape(128, 8192))

    shared = {
        "wqkv_k": wqkv_k, "wqkv_v": wqkv_v, "wqkv_q": wqkv_q, "wproj": wproj_cm,
        "lamv": lamv, "consts": consts, "ident": ident,
        "gate_pk3": gate_pk3, "up_pk3": up_pk3, "down_pk3": down_pk3,
        "gu_sc": gu_sc, "ds_sc": ds_sc,
    }

    kt = np.arange(128)
    qt = np.arange(256)
    in_maps = []
    for c in range(N_CORES):
        b, c4 = divmod(c, 4)
        tiles = TILES[c4]
        m = np.zeros((32, 128, 512), ml_dtypes.bfloat16)
        for jt in range(4):
            for i8 in range(8):
                kc = PC[jt] - 8 + i8
                m1 = ((128 * kc + kt[:, None])
                      <= (256 * tiles[jt] + qt[None, :256])).astype(np.float32)
                m[jt * 8 + i8] = np.concatenate([m1, m1], 1)
        d = dict(shared)
        d["x_kv"] = np.ascontiguousarray(x[b])
        d["x_q"] = np.ascontiguousarray(
            np.concatenate([x[b, 256 * t:256 * (t + 1)] for t in tiles], 0))
        d["masks"] = np.ascontiguousarray(m.transpose(1, 0, 2).reshape(128, 32 * 512))
        in_maps.append(d)
    return in_maps



def _aslist(w):
    if w is None:
        return []
    if isinstance(w, tuple) and len(w) == 2 and isinstance(w[0], str):
        return [w]
    out = []
    for m in w:
        out.extend(_aslist(m))
    return out


class Prog:
    ENGINES = ("tensor", "vector", "scalar", "gpsimd", "sync")

    def __init__(self, nc, ctx):
        self.nc = nc
        self.ctx = ctx
        self.body = {e: [] for e in self.ENGINES}
        self.sems = {}
        self.waited = {e: {} for e in self.ENGINES}

    def sem(self, name):
        if name not in self.sems:
            h = self.ctx.enter_context(self.nc.semaphore(name))
            self.sems[name] = [h, 0]
        return self.sems[name]

    def wait(self, eng, mark):
        for m in _aslist(mark):
            if m[1] <= 0 or self.waited[eng].get(m[0], 0) >= m[1]:
                continue
            self.waited[eng][m[0]] = m[1]
            h = self.sems[m[0]][0]
            self.body[eng].append(lambda e, h=h, v=m[1]: e.wait_ge(h, v))

    def op(self, eng, fname, *a, inc=None, **k):
        if inc is not None:
            s = self.sem(inc[0])
            s[1] += inc[1]
            h, val = s[0], s[1]

            def run(e, fname=fname, a=a, k=k, h=h, amt=inc[1]):
                getattr(e, fname)(*a, **k).then_inc(h, amt)
            self.body[eng].append(run)
            return (inc[0], val)

        def run(e, fname=fname, a=a, k=k):
            getattr(e, fname)(*a, **k)
        self.body[eng].append(run)
        return None

    def mark(self, name):
        return (name, self.sems[name][1]) if name in self.sems else (name, 0)


SELF_SERIAL = True


class SerialEngine:
    def __init__(self, prog, eng, semname, serial=True):
        self.p, self.eng, self.semname = prog, eng, semname
        self.serial = serial
        self.p.sem(semname)

    def __init2__(self):
        pass

    def op(self, fname, *a, waits=(), noserial=False, **k):
        self.p.wait(self.eng, waits)
        if self.serial and not noserial:
            n = self.p.sems[self.semname][1]
            if n > 0:
                self.p.wait(self.eng, (self.semname, n))
        return self.p.op(self.eng, fname, *a, inc=(self.semname, 1), **k)

    def mark(self):
        return self.p.mark(self.semname)


class DmaQueue:
    def __init__(self, prog, eng):
        self.p, self.eng = prog, eng

    def dma(self, semname, out, in_, waits=(), transpose=False):
        self.p.wait(self.eng, waits)
        n = self.p.sems[semname][1] if semname in self.p.sems else 0
        if n > 0:
            self.p.wait(self.eng, (semname, n))
        if transpose:
            return self.p.op(self.eng, "dma_start_transpose", out, in_,
                             inc=(semname, 16))
        return self.p.op(self.eng, "dma_start", out=out, in_=in_,
                         inc=(semname, 16))

    def dma_group(self, semname, pairs, waits=()):
        self.p.wait(self.eng, waits)
        for out, in_ in pairs:
            self.p.op(self.eng, "dma_start", out=out, in_=in_, inc=(semname, 16))
        return self.p.mark(semname)


class TensorEng:
    def __init__(self, prog):
        self.p = prog
        self.rot = 0
        self.p.sem("st0")
        self.p.sem("st1")

    def wait(self, mark):
        self.p.wait("tensor", mark)

    def mm(self, *a, **k):
        self.p.op("tensor", "matmul", *a, **k)

    def mile(self, fname, *a, **k):
        name = f"st{self.rot}"
        self.rot ^= 1
        n = self.p.sems[name][1]
        if n > 0:
            self.p.wait("tensor", (name, n))
        return self.p.op("tensor", fname, *a, inc=(name, 1), **k)

    def fence(self):
        return [self.p.mark("st0"), self.p.mark("st1")]



def build_and_run(in_maps, trace=False):
    from contextlib import ExitStack
    nc = bass.Bass("TRN2", target_bir_lowering=False)

    di = lambda n, s, dt: nc.dram_tensor(n, s, dt, kind="ExternalInput")
    dd = lambda n, s, dt: nc.dram_tensor(n, s, dt, kind="Internal")
    x_kv = di("x_kv", [T, D], F32)
    x_q = di("x_q", [NQ, D], F32)
    wqkv_k = di("wqkv_k", [128, 16 * 2048], F32)
    wqkv_v = di("wqkv_v", [128, 16 * 2048], F32)
    wqkv_q = di("wqkv_q", [128, 16 * 2048], F32)
    wproj = di("wproj", [128, 16 * 2048], F32)
    lamv = di("lamv", [128, 256], F32)
    consts = di("consts", [128, 20], F32)
    ident = di("ident", [128, 128], BF16)
    masks = di("masks", [128, 32 * 512], BF16)
    gate_pk3 = di("gate_pk3", [32, 128, 4096], U8)
    up_pk3 = di("up_pk3", [32, 128, 4096], U8)
    down_pk3 = di("down_pk3", [32, 128, 4096], U8)
    gu_sc = di("gu_sc", [128, NF], F32)
    ds_sc = di("ds_sc", [128, 8192], F32)
    out = nc.dram_tensor("out", [NQ, D], F32, kind="ExternalOutput")

    xn_kv_d = dd("xn_kv_d", [T, D], BF16)
    xn_q_d = dd("xn_q_d", [NQ, D], BF16)
    xnqT_d = dd("xnqT_d", [D, NQ], BF16)
    k12T_d = dd("k12T_d", [NH, 128, T], BF16)
    vext_d = dd("vext_d", [NH, 128, 32, 132], BF16)
    q12T_d = dd("q12T_d", [NH, 128, NQ], BF16)
    s_d = dd("s_d", [NQ, D], F32)
    attnT_d = dd("attnT_d", [8, 128, NH, 128], BF16)
    inter_d = dd("inter_d", [128, 128, NQ], BF16)
    wdT_dram = dd("wdT_dram", [32, 128, 8192], BF16)

    ctx = ExitStack()
    with ctx:
        sb = lambda n, s, dt: ctx.enter_context(nc.sbuf_tensor(n, s, dt))
        psal = lambda n, s, dt: ctx.enter_context(nc.psum_tensor(n, s, dt))

        AR0 = sb("AR0", [128, 65536], U8)
        AR1 = sb("AR1", [128, 65536], U8)
        AR2 = sb("AR2", [128, 49152], U8)
        AR3 = sb("AR3", [128, 22528], U8)

        def v8(ar, off, nbytes, dt, cols):
            return ar[:, off:off + nbytes].bitcast(dt)[:, 0:cols]

        wk_sb = v8(AR0, 0, 65536, BF16, 16 * 2048)
        wv_sb = wk_sb
        wq_sb = wk_sb
        wp_sb = wk_sb

        xt = [v8(AR1, 0, 8192, F32, D), v8(AR1, 8192, 8192, F32, D)]
        lnsq = v8(AR1, 16384, 8192, F32, D)
        xnt = [v8(AR1, 24576, 4096, BF16, D), v8(AR1, 28672, 4096, BF16, D)]
        xnT_sb = [v8(AR2, 0, 16384, BF16, 16 * 512),
                  v8(AR2, 16384, 16384, BF16, 16 * 512)]
        qkv_ev = [v8(AR1, 0, 1024, BF16, 512), v8(AR1, 1024, 1024, BF16, 512)]
        vtr_ev = [v8(AR1, 2048, 1024, BF16, 512), v8(AR1, 3072, 1024, BF16, 512)]
        vex_ev = [v8(AR1, 4096, 1088, BF16, 4 * 132), v8(AR1, 5248, 1088, BF16, 4 * 132)]

        k12 = [v8(AR2, 0, 8192, BF16, T), v8(AR2, 8192, 8192, BF16, T)]
        vex = [v8(AR2, 16384, 8448, BF16, 32 * 132), v8(AR2, 24832, 8448, BF16, 32 * 132)]
        q12 = [v8(AR2, 33280, 2048, BF16, NQ), v8(AR2, 35328, 2048, BF16, NQ)]
        att_qs = [v8(AR2, 0, 4096, BF16, D), v8(AR2, 4096, 4096, BF16, D)]
        s_ev = [v8(AR2, 8192, 8192, F32, D)]

        cst = sb("cst", [128, 20], F32)
        for i, v in enumerate(CONST_VALS):
            nc.const_aps.register(F32, v, cst[:, i:i + 1])
        lam_sb = sb("lam_sb", [128, 8], F32)
        lamv_sb = v8(AR1, 16384, 1024, F32, 256)
        lamx = v8(AR1, 20480, 1024, F32, 256)
        id_b = sb("id_b", [128, 128], BF16)
        lnst = sb("lnst", [128, 8], F32)
        ep_st = sb("ep_st", [128, 16], F32)
        p12 = [sb(f"p12_{i}", [128, 1024], BF16) for i in range(2)]
        scr = [sb(f"scr{i}", [128, 256], F32) for i in range(4)]
        ep_t2 = scr[0][:, 0:128]
        ep_diff = scr[1][:, 0:128]
        ep_sq = scr[2][:, 0:128]
        ep_on = scr[3][:, 0:128].bitcast(BF16)[:, 0:128]
        gl_sq, gl_v2, gl_v3, gl_t = (s[:] for s in scr)
        gl_up = scr[2][:]
        att_st = [sb(f"att_st{i}", [128, 128], BF16) for i in range(2)]

        ps_ab = psal("ps_ab", [128, 1024], F32)
        ps_cd = psal("ps_cd", [128, 1024], F32)
        ps_big = psal("ps_big", [128, 2048], F32)

        class _PV:
            def __init__(self, t, lo):
                self.t, self.lo = t, lo
            def __getitem__(self, key):
                assert isinstance(key, tuple) or key == slice(None)
                if not isinstance(key, tuple):
                    return self.t[:, self.lo:self.lo + 512]
                p, f = key
                assert p == slice(None)
                return self.t[:, self.lo + (f.start or 0):self.lo + f.stop]
        ps_a = _PV(ps_ab, 0)
        ps_b = _PV(ps_ab, 512)
        ps_c = _PV(ps_cd, 0)
        ps_d = _PV(ps_cd, 512)
        ps4 = [ps_a, ps_b, ps_c, ps_d]

        p = Prog(nc, ctx)
        V = SerialEngine(p, "vector", "sv")
        S = SerialEngine(p, "scalar", "ss", serial=False)
        G = SerialEngine(p, "gpsimd", "sg", serial=False)
        TE = TensorEng(p)
        Dsy = DmaQueue(p, "sync")
        Dsc = DmaQueue(p, "scalar")
        Dgp = DmaQueue(p, "gpsimd")

        m_ini = Dsy.dma_group("ld_ini", [
            (cst[:], consts.ap()),
            (id_b[:], ident.ap()),
            (lamv_sb, lamv.ap()),
        ])
        m_wk = Dgp.dma("ld_w0", wk_sb, wqkv_k.ap())
        ds_sc_sb = v8(AR1, 32768, 16384, BF16, 8192)
        m_dssc = Dgp.dma("ld_dssc", ds_sc_sb, ds_sc.ap())

        S.op("copy", lam_sb[:, 1:2], cst[:, 0:1], waits=[m_ini])

        V.op("tensor_tensor", lamx[:, 0:128], lamv_sb[:, 0:128],
             lamv_sb[:, 128:256], ALU.mult, waits=[m_ini])
        V.op("reduce_sum", lam_sb[:, 2:3], lamx[:, 0:64], axis=AX.X)
        m_d12 = V.op("reduce_sum", lam_sb[:, 3:4], lamx[:, 64:128], axis=AX.X)
        S.op("activation", lam_sb[:, 4:5], lam_sb[:, 2:3], AF.Exp,
             waits=[m_d12])
        m_e2 = S.op("activation", lam_sb[:, 5:6], lam_sb[:, 3:4], AF.Exp)
        V.op("tensor_tensor", lam_sb[:, 6:7], lam_sb[:, 4:5],
             lam_sb[:, 5:6], ALU.subtract, waits=[m_e2])
        m_lam = V.op("tensor_scalar", lam_sb[:, 0:1], lam_sb[:, 6:7],
                     LAMBDA_INIT, None, ALU.add)

        sp_xn_mark = [None, None]
        ln_v_mark = [None, None]
        for i in range(40):
            par = i % 2
            if i < 32:
                src_ = x_kv.ap()[128 * i:128 * (i + 1), :]
                dst = xn_kv_d.ap()[128 * i:128 * (i + 1), :]
            else:
                j = i - 32
                src_ = x_q.ap()[128 * j:128 * (j + 1), :]
                dst = xn_q_d.ap()[128 * j:128 * (j + 1), :]
            m_ld = Dsy.dma(f"ld_x{par}", xt[par], src_,
                           waits=[ln_v_mark[par], m_lam if i == 0 else None])
            V.op("reduce_sum", lnst[:, 0:1], xt[par], axis=AX.X,
                 waits=[m_ld])
            m_sq = S.op("activation", lnsq, xt[par], AF.Square,
                        accum_out=lnst[:, 1:2], waits=[m_ld, V.mark()])
            V.op("tensor_scalar", lnst[:, 2:3], lnst[:, 0:1], 1.0 / D, None,
                 ALU.mult, waits=[m_sq])
            V.op("tensor_tensor", lnst[:, 3:4], lnst[:, 2:3], lnst[:, 2:3],
                 ALU.mult)
            V.op("tensor_scalar", lnst[:, 4:5], lnst[:, 1:2], 1.0 / D, EPS,
                 ALU.mult, ALU.add)
            m_var = V.op("tensor_tensor", lnst[:, 5:6], lnst[:, 4:5],
                         lnst[:, 3:4], ALU.subtract)
            S.op("activation", lnst[:, 6:7], lnst[:, 5:6], AF.Ln,
                 waits=[m_var])
            m_rstd = S.op("activation", lnst[:, 7:8], lnst[:, 6:7], AF.Exp,
                          scale=-0.5)
            m_xn = V.op("tensor_scalar", xnt[par], xt[par],
                        lnst[:, 2:3], lnst[:, 7:8], ALU.subtract, ALU.mult,
                        waits=[m_rstd, sp_xn_mark[par]])
            ln_v_mark[par] = m_xn
            sp_xn_mark[par] = Dsc.dma(f"sp_xn{par}", dst, xnt[par],
                                      waits=[m_xn])
        m_ln_done = [sp_xn_mark[0], sp_xn_mark[1]]
        m_ln_v = V.mark()

        adl = [dict(c4=v8(AR1, 49152, 4096, I16, 2048),
                    s15=v8(AR1, 53248, 4096, I16, 2048),
                    xb=v8(AR1, 57344, 4096, I16, 2048),
                    xbf=v8(AR1, 57344, 4096, BF16, 2048),
                    r=v8(AR1, 61440, 4096, BF16, 2048),
                    pk=v8(AR3, 0, 1024, U8, 1024),
                    sc=v8(AR3, 1024, 2048, BF16, 1024)),
               dict(c4=v8(AR3, 3072, 4096, I16, 2048),
                    s15=v8(AR3, 7168, 4096, I16, 2048),
                    xb=v8(AR3, 11264, 4096, I16, 2048),
                    xbf=v8(AR3, 11264, 4096, BF16, 2048),
                    r=v8(AR3, 15360, 4096, BF16, 2048),
                    pk=v8(AR3, 19456, 1024, U8, 1024),
                    sc=v8(AR3, 20480, 2048, BF16, 1024))]
        ad_pm = [dict(), dict()]
        wdsp = {}
        ad_state = {"i1": 0, "i2": 0}

        def adq_s1(u):
            Jp, cc = u // 2, u % 2
            L = u % 2
            t = adl[L]
            pm = ad_pm[L]
            b, j = Jp // 2, Jp % 2
            m_pk = Dsc.dma(f"ld_pkD{L}", t["pk"],
                           down_pk3.ap()[b, :, 1024 * (2 * j + cc):
                                         1024 * (2 * j + cc + 1)],
                           waits=[pm.get("chi"), pm.get("clo")])
            p0 = (8 * Jp) % 128
            c0 = 2048 * (Jp // 16) + 1024 * cc
            src_sc = (ds_sc_sb[p0:p0 + 8, c0:c0 + 1024]
                      .unsqueeze(1).broadcast_to([8, 16, 1024]))
            m_sc = Dgp.dma(f"ld_scD{L}", t["sc"], src_sc,
                           waits=[pm.get("mul1"),
                                  m_dssc if u < 2 else None])
            m_chi = S.op("activation", t["c4"][:, 1024:2048], t["pk"],
                         AF.Identity, scale=0.0625, bias=-0.499999,
                         waits=[m_pk, pm.get("xb")])
            m_clo = V.op("scalar_tensor_tensor", t["c4"][:, 0:1024],
                         t["c4"][:, 1024:2048], -16.0, t["pk"],
                         ALU.mult, ALU.add, noserial=True,
                         waits=[m_chi, pm.get("xb")])
            m_s15 = V.op("tensor_scalar", t["s15"], t["c4"], 3, 15,
                         ALU.logical_shift_right, ALU.logical_shift_left,
                         noserial=True, waits=[m_clo, pm.get("orr")])
            m_and = V.op("tensor_single_scalar", t["c4"], t["c4"], 7,
                         ALU.bitwise_and, noserial=True, waits=[m_s15])
            m_xb = V.op("tensor_scalar", t["xb"], t["c4"], 64, 16128,
                        ALU.mult, ALU.add, noserial=True,
                        waits=[m_and] + _aslist(pm.get("sp")))
            pm.update(chi=m_chi, clo=m_clo, s15=m_s15, xb=m_xb, sc=m_sc)

        def adq_s2(u):
            Jp, cc = u // 2, u % 2
            L = u % 2
            t = adl[L]
            pm = ad_pm[L]
            b, j = Jp // 2, Jp % 2
            m_r = S.op("activation", t["r"], t["xb"], AF.Relu, bias=63.5,
                       scale=-1.0 / 256.0, waits=[pm["xb"], pm.get("val")])
            m_val = V.op("tensor_tensor", t["xbf"], t["xbf"], t["r"],
                         ALU.subtract, noserial=True, waits=[m_r])
            m_or = V.op("tensor_tensor", t["xb"], t["xb"], t["s15"],
                        ALU.add, noserial=True, waits=[m_val])
            m_m0 = V.op("tensor_tensor", t["xbf"][:, 0:1024],
                        t["xbf"][:, 0:1024], t["sc"], ALU.mult,
                        noserial=True, waits=[m_or, pm["sc"]])
            m_m1 = V.op("tensor_tensor", t["xbf"][:, 1024:2048],
                        t["xbf"][:, 1024:2048], t["sc"], ALU.mult,
                        noserial=True, waits=[m_m0])
            o0 = 4096 * cc + 2048 * j
            m_spa = Dsc.dma(f"sp_wd{L}a",
                            wdT_dram.ap()[b, :, o0:o0 + 1024],
                            t["xbf"][:, 0:1024], waits=[m_m0])
            m_spb = Dsc.dma(f"sp_wd{L}b",
                            wdT_dram.ap()[b, :, o0 + 1024:o0 + 2048],
                            t["xbf"][:, 1024:2048], waits=[m_m1])
            pm.update(r=m_r, val=m_val, orr=m_or, mul1=m_m1,
                      sp=[m_spa, m_spb])
            wdsp.setdefault((b, cc), []).extend([m_spa, m_spb])

        def emit_adq():
            if ad_state["i1"] < 128:
                adq_s1(ad_state["i1"])
                ad_state["i1"] += 1
            lim = ad_state["i1"] - (1 if ad_state["i1"] < 128 else 0)
            if ad_state["i2"] < lim:
                adq_s2(ad_state["i2"])
                ad_state["i2"] += 1

        qkv_ps = [ps_a, ps_c]
        ps_war = [None, None]
        tr_bank_war = [None, None]
        xnT_war = [None, None]
        kq_war = [None, None]
        vtr_war = [None, None]
        vexev_war = [None, None]
        k_sp = v_sp = q_sp = None
        m_vex_init = None

        def qkv_pass(tiles, xn_src, m_w, mode):
            nonlocal k_sp, v_sp, q_sp, m_vex_init
            for jj, j in enumerate(tiles):
                par = jj % 2
                t0 = (j - 8) * 512 if mode == "q" else j * 512
                wlist = list(m_ln_done) + _aslist(xnT_war[par])
                for c0 in range(16):
                    Dsy.dma(f"ld_xnT{par}",
                            xnT_sb[par][:, 512 * c0:512 * (c0 + 1)],
                            xn_src.ap()[t0:t0 + 512, 128 * c0:128 * (c0 + 1)],
                            waits=wlist if c0 == 0 else (), transpose=True)
                m_xnT = p.mark(f"ld_xnT{par}")
                if mode == "q":
                    for c0 in range(16):
                        Dsc.dma("sp_xnqT",
                                xnqT_d.ap()[128 * c0:128 * (c0 + 1), t0:t0 + 512],
                                xnT_sb[par][:, 512 * c0:512 * (c0 + 1)],
                                waits=[m_xnT] if c0 == 0 else ())
                TE.wait(m_xnT)
                TE.wait(m_w)
                for cc in range(16):
                    ppi = cc % 2
                    pp = qkv_ps[ppi]
                    TE.wait(ps_war[ppi])
                    m_mm = None
                    for hc in range(16):
                        kw = dict(start=(hc == 0), stop=(hc == 15))
                        lhsT = wk_sb[:, 2048 * hc + 128 * cc:
                                     2048 * hc + 128 * (cc + 1)]
                        rhs = xnT_sb[par][:, 512 * hc:512 * (hc + 1)]
                        if hc == 15:
                            m_mm = TE.mile("matmul", pp[:], lhsT, rhs, **kw)
                        else:
                            TE.mm(pp[:], lhsT, rhs, **kw)
                    bufp = cc % 2
                    if mode in ("k", "q"):
                        m_ev = S.op("copy", qkv_ev[bufp], pp[:],
                                    waits=[m_mm, kq_war[bufp],
                                           m_ln_v if (jj == 0 and cc < 2) else None])
                        ps_war[ppi] = m_ev
                        if mode == "q":
                            m_sp = Dsc.dma("sp_q", q12T_d.ap()[cc, :, t0:t0 + 512],
                                           qkv_ev[bufp], waits=[m_ev])
                            q_sp = m_sp
                        else:
                            m_sp = Dsc.dma("sp_k", k12T_d.ap()[cc, :, t0:t0 + 512],
                                           qkv_ev[bufp], waits=[m_ev])
                            k_sp = m_sp
                        kq_war[bufp] = m_sp
                    else:
                        h = cc
                        m_ev = S.op("copy", vtr_ev[bufp], pp[:],
                                    waits=[m_mm, vtr_war[bufp]])
                        ps_war[ppi] = m_ev
                        cp_marks = [None] * 4
                        for kk in range(4):
                            bank = [ps_b, ps_d][kk % 2]
                            dstps = bank[:, 64 * (kk // 2):
                                         64 * (kk // 2 + 1)].bitcast(BF16)
                            if kk == 0:
                                TE.wait(m_ev)
                                TE.wait(m_ini)
                            TE.wait(cp_marks[kk - 2] if kk >= 2
                                    else tr_bank_war[kk % 2])
                            m_tr = TE.mile("transpose", dstps,
                                           vtr_ev[bufp][:, 128 * kk:128 * (kk + 1)],
                                           id_b[:])
                            cp_marks[kk] = S.op(
                                "copy", vex_ev[bufp][:, 132 * kk:132 * kk + 128],
                                dstps,
                                waits=[m_tr,
                                       vexev_war[bufp] if kk == 0 else None,
                                       m_vex_init if kk == 0 else None])
                        tr_bank_war[0] = cp_marks[2]
                        tr_bank_war[1] = cp_marks[3]
                        vtr_war[bufp] = cp_marks[3]
                        m_sp = Dsc.dma(
                            "sp_v", vext_d.ap()[h, :, 4 * j:4 * (j + 1), :],
                            vex_ev[bufp].rearrange("p (a f) -> p a f", a=4),
                            waits=[cp_marks[3]])
                        vexev_war[bufp] = m_sp
                        v_sp = m_sp
                    if cc % 4 == 1:
                        emit_adq()
                xnT_war[par] = TE.fence()

        m_vex_init = None
        for i2 in range(2):
            V.op("memset", vex_ev[i2], 0.0, waits=[m_ln_v])
            for kk in range(4):
                m_vex_init = V.op("memset",
                                  vex_ev[i2][:, 132 * kk + 128:132 * kk + 129],
                                  1.0)
        qkv_pass(list(range(8)), xn_kv_d, m_wk, "k")
        m_wv = Dgp.dma("ld_w1", wk_sb, wqkv_v.ap(), waits=TE.fence())
        qkv_pass(list(range(8)), xn_kv_d, m_wv, "v")
        m_wq2 = Dgp.dma("ld_w2", wk_sb, wqkv_q.ap(), waits=TE.fence())
        qkv_pass([8, 9], xn_q_d, m_wq2, "q")
        m_wp = Dgp.dma("ld_w3", wk_sb, wproj.ap(), waits=TE.fence())

        def off(mat, sub):
            return (sub // 2 + 2 * mat) * 512 + (sub % 2) * 129

        m_qkv_done = [p.mark("sp_k"), p.mark("sp_v"), p.mark("sp_q"),
                      p.mark("sp_xnqT")]
        msk_sb = v8(AR1, 0, 32768, BF16, 32 * 512)
        m_msk = Dsy.dma("ld_msk", msk_sb, masks.ap(), waits=m_qkv_done)
        m_msk_last = None

        hd_war = [None, None]
        exp_war = {}
        av_war = {}
        ep_copies = []
        att_sp = [None, None]
        first_att = True
        OBASE = [0, 512, 1024, 1536]

        for h in range(NH):
            hb = h % 2
            m_k = Dsy.dma(f"ld_k{hb}", k12[hb], k12T_d.ap()[h],
                          waits=[k_sp, v_sp, q_sp] + _aslist(hd_war[hb]))
            m_v = Dsy.dma(f"ld_v{hb}",
                          vex[hb].rearrange("p (a f) -> p a f", a=32),
                          vext_d.ap()[h])
            m_q = Dsy.dma(f"ld_q{hb}", q12[hb], q12T_d.ap()[h])
            for jt in range(4):
                prev_ep = list(ep_copies)
                ep_copies = []
                msk_marks = {}
                last = None

                def emit_scores(kc):
                    sp = kc % 2
                    pst = [ps_ab, ps_cd][sp]
                    pbuf = p12[(kc % 4) // 2][:, 512 * sp:512 * (sp + 1)]
                    if kc == 0:
                        TE.wait(m_k)
                        TE.wait(m_q)
                        TE.wait(m_v)
                        for mk in prev_ep:
                            TE.wait(mk)
                    if first_att and kc < 2:
                        TE.wait(ps_war[sp])
                        TE.wait(tr_bank_war[0])
                        TE.wait(tr_bank_war[1])
                    TE.wait(exp_war.get(sp))
                    TE.mm(pst[:, 0:256], k12[hb][0:64, 128 * kc:128 * (kc + 1)],
                          q12[hb][0:64, 256 * jt:256 * (jt + 1)],
                          start=True, stop=True)
                    m_s2 = TE.mile("matmul", pst[:, 512:768],
                                   k12[hb][64:128, 128 * kc:128 * (kc + 1)],
                                   q12[hb][64:128, 256 * jt:256 * (jt + 1)],
                                   start=True, stop=True)
                    sview = pst[:].rearrange("p (b c) -> p b c", b=2)[:, :, 0:256]
                    m_exp = S.op("activation",
                                 pbuf.rearrange("p (b c) -> p b c", b=2),
                                 sview, AF.Exp, bias=-12.0, scale=0.125,
                                 waits=[m_s2, av_war.get(kc % 4)])
                    exp_war[sp] = m_exp
                    base = PC[jt] - 8
                    if kc >= base:
                        mslice = msk_sb[:, (jt * 8 + kc - base) * 512:
                                        (jt * 8 + kc - base + 1) * 512]
                        m2 = V.op("tensor_tensor", pbuf, pbuf, mslice,
                                  ALU.mult, noserial=True,
                                  waits=[m_exp, m_msk, av_war.get(kc % 4)])
                        msk_marks[kc] = m2
                    else:
                        msk_marks[kc] = m_exp

                def emit_av(kc):
                    nonlocal last
                    sp = kc % 2
                    pbuf = p12[(kc % 4) // 2][:, 512 * sp:512 * (sp + 1)]
                    TE.wait(msk_marks[kc])
                    for mat in range(2):
                        for sub in range(2):
                            o = OBASE[2 * mat + sub]
                            kw = dict(start=(kc == 0), stop=(kc == PC[jt] - 1))
                            lhsT = pbuf[:, 256 * mat + 128 * sub:
                                        256 * mat + 128 * (sub + 1)]
                            rhs = vex[hb][:, 132 * kc:132 * kc + 129]
                            if (mat, sub) == (1, 1):
                                last = TE.mile("matmul", ps_big[:, o:o + 129],
                                               lhsT, rhs, **kw)
                            else:
                                TE.mm(ps_big[:, o:o + 129], lhsT, rhs, **kw)
                    av_war[kc % 4] = last

                for kc in range(PC[jt] + 2):
                    if kc < PC[jt]:
                        emit_scores(kc)
                    if kc >= 2:
                        emit_av(kc - 2)
                m_msk_last = msk_marks[PC[jt] - 1]
                first_att = False
                m_av_done = last
                eps = []
                for sub in range(2):
                    c0 = 8 * sub
                    o1 = OBASE[sub]
                    o2 = OBASE[2 + sub]
                    t2v = scr[0][:, 128 * sub:128 * (sub + 1)]
                    dfv = scr[1][:, 128 * sub:128 * (sub + 1)]
                    sqv = scr[2][:, 128 * sub:128 * (sub + 1)]
                    onv = (scr[3][:, 128 * sub:128 * (sub + 1)]
                           .bitcast(BF16)[:, 0:128])
                    V.op("reciprocal", ep_st[:, c0:c0 + 1],
                         ps_big[:, o1 + 128:o1 + 129],
                         waits=[m_av_done, m_lam] if sub == 0 else None)
                    V.op("reciprocal", ep_st[:, c0 + 1:c0 + 2],
                         ps_big[:, o2 + 128:o2 + 129])
                    V.op("tensor_tensor", ep_st[:, c0 + 2:c0 + 3],
                         ep_st[:, c0 + 1:c0 + 2],
                         lam_sb[:, 0:1], ALU.mult)
                    V.op("tensor_scalar", t2v,
                         ps_big[:, o2:o2 + 128],
                         ep_st[:, c0 + 2:c0 + 3], None, ALU.mult)
                    m_diff = V.op("scalar_tensor_tensor", dfv,
                                  ps_big[:, o1:o1 + 128],
                                  ep_st[:, c0:c0 + 1], t2v, ALU.mult,
                                  ALU.subtract)
                    S.op("activation", sqv, dfv, AF.Square,
                         accum_out=ep_st[:, c0 + 3:c0 + 4], waits=[m_diff])
                    S.op("activation", ep_st[:, c0 + 4:c0 + 5],
                         ep_st[:, c0 + 3:c0 + 4], AF.Ln,
                         bias=EPS, scale=1.0 / VD)
                    m_rs = S.op("activation", ep_st[:, c0 + 5:c0 + 6],
                                ep_st[:, c0 + 4:c0 + 5],
                                AF.Exp, bias=math.log(ONE_M_LI), scale=-0.5)
                    eps.append((m_rs, dfv, onv, c0))
                for sub in range(2):
                    m_rs, dfv, onv, c0 = eps[sub]
                    stp = sub % 2
                    m_on = V.op("tensor_scalar", onv, dfv,
                                ep_st[:, c0 + 5:c0 + 6], None, ALU.mult,
                                waits=[m_rs])
                    bank = [ps_b, ps_d][stp]
                    dstps = bank[:, 384:448].bitcast(BF16)
                    TE.wait(m_on)
                    TE.wait(tr_bank_war[stp])
                    m_tr = TE.mile("transpose", dstps, onv, id_b[:])
                    m_cp = V.op("tensor_copy", att_st[stp][:], dstps,
                                waits=[m_tr, att_sp[stp]])
                    qs = 2 * jt + sub
                    att_sp[stp] = Dsc.dma("sp_att",
                                          attnT_d.ap()[qs, :, h, :],
                                          att_st[stp][:], waits=[m_cp])
                    ep_copies.append(m_cp)
                    tr_bank_war[stp] = m_cp
                emit_adq()
            hd_war[hb] = TE.fence()
        while ad_state["i2"] < 128:
            emit_adq()

        gu_sc_sb = v8(AR2, 16384, 32768, BF16, NF)
        m_gusc = Dgp.dma("ld_gusc", gu_sc_sb, gu_sc.ap(),
                         waits=_aslist([hd_war[0], hd_war[1]]))

        m_att_all = p.mark("sp_att")
        aq_war = [None, None]
        sev_war = None
        m_sp_s = None
        prev_guard = list(ep_copies)
        hd_fence = TE.fence()
        for qs in range(8):
            bufp = qs % 2
            m_aq = Dsy.dma(f"ld_aq{bufp}",
                           att_qs[bufp].rearrange("p (a f) -> p a f", a=16),
                           attnT_d.ap()[qs],
                           waits=[m_att_all] + (hd_fence if qs < 2 else [])
                                 + _aslist(aq_war[bufp]))
            TE.wait(m_aq)
            if qs == 0:
                TE.wait(m_wp)
            for mk in prev_guard:
                TE.wait(mk)
            m_pj = None
            for h in range(NH):
                for c4 in range(4):
                    kw = dict(start=(h == 0), stop=(h == 15))
                    lhsT = att_qs[bufp][:, 128 * h:128 * (h + 1)]
                    rhs = wk_sb[:, 2048 * h + 512 * c4:2048 * h + 512 * (c4 + 1)]
                    if (h, c4) == (15, 3):
                        m_pj = TE.mile("matmul",
                                       ps_big[:, 512 * c4:512 * (c4 + 1)],
                                       lhsT, rhs, **kw)
                    else:
                        TE.mm(ps_big[:, 512 * c4:512 * (c4 + 1)], lhsT, rhs,
                              **kw)
            aq_war[bufp] = TE.fence()
            m_ev = V.op("tensor_copy", s_ev[0], ps_big[:],
                        waits=[m_pj, sev_war])
            m_sp_s = Dsc.dma("sp_s", s_d.ap()[128 * qs:128 * (qs + 1), :],
                             s_ev[0], waits=[m_ev])
            sev_war = m_sp_s
            prev_guard = [m_ev]
        m_proj_ev = prev_guard[0]

        m_p5_te = TE.fence()
        mb_v = V.mark()
        mb_s = S.mark()
        m_spwd = [p.mark("sp_wd0a"), p.mark("sp_wd0b"),
                  p.mark("sp_wd1a"), p.mark("sp_wd1b")]
        m_scD = [p.mark("ld_scD0"), p.mark("ld_scD1")]

        xnqA = v8(AR1, 0, 32768, BF16, 16 * NQ)
        glA = [v8(AR1, 32768 + 4096 * i, 4096, F32, NQ) for i in range(4)]
        c4A = [v8(AR3, 16384, 2048, I16, 1024),
               v8(AR1, 49152, 2048, I16, 1024)]
        s15A = [v8(AR3, 18432, 2048, I16, 1024),
                v8(AR1, 51200, 2048, I16, 1024)]
        rA = [v8(AR3, 20480, 2048, BF16, 1024),
              v8(AR1, 61440, 2048, BF16, 1024)]
        scb2A = [v8(AR1, 53248 + 1024 * i, 1024, BF16, 512) for i in range(4)]
        ievA = [v8(AR1, 57344, 2048, BF16, NQ), v8(AR1, 59392, 2048, BF16, NQ)]
        pkA = [[v8(AR3, 0, 4096, U8, 4096), v8(AR3, 4096, 4096, U8, 4096)],
               [v8(AR3, 8192, 4096, U8, 4096), v8(AR3, 12288, 4096, U8, 4096)]]
        gubuf = [[v8(AR0, 0, 16384, BF16, 8192),
                  v8(AR0, 16384, 16384, BF16, 8192)],
                 [v8(AR0, 32768, 16384, BF16, 8192),
                  v8(AR0, 49152, 16384, BF16, 8192)]]
        gubuf_i = [[v8(AR0, 0, 16384, I16, 8192),
                    v8(AR0, 16384, 16384, I16, 8192)],
                   [v8(AR0, 32768, 16384, I16, 8192),
                    v8(AR0, 49152, 16384, I16, 8192)]]

        for g in range(16):
            J2, par2 = divmod(g, 2)
            src_ = xnqT_d.ap()[256 * J2 + par2:256 * J2 + 256:2, 0:NQ]
            w0 = [p.mark("sp_xnqT"), m_msk_last] if g == 0 else []
            Dsy.dma("ld_xnq", xnqA[:, NQ * g:NQ * (g + 1)], src_, waits=w0)
        m_xnq = p.mark("ld_xnq")

        pmA = [dict(xb=[mb_v, mb_s], orr=mb_v, mul1=mb_v, val=m_spwd),
               dict(xb=[mb_v, mb_s], orr=mb_v, mul1=mb_v, val=m_spwd)]
        guready = {}
        fenceA = {}
        pkA_marks = {}
        ispill = {}
        ievA_war = [[mb_v] + m_spwd, [mb_v] + m_spwd]
        psA_war = {0: [mb_v, mb_s], 1: [mb_v, mb_s]}
        sqA_war = [[mb_v] + m_scD]

        scparA = [[mb_v, mb_v], [mb_v, mb_v]]

        def ffnA_s1(s, u):
            mat, J = u % 2, u // 2
            pm = pmA[mat]
            dbuf = s % 2
            pk = pkA[mat][dbuf][:, 512 * J:512 * (J + 1)]
            c4, s15 = c4A[mat], s15A[mat]
            scb2 = scb2A[2 * mat + J % 2]
            src_sc = (gu_sc_sb[64 * mat + 8 * J:64 * mat + 8 * J + 8,
                               512 * s:512 * (s + 1)]
                      .unsqueeze(1).broadcast_to([8, 16, 512]))
            m_sc = Dsc.dma(f"ld_scA{mat}{J % 2}", scb2, src_sc,
                           waits=[scparA[mat][J % 2],
                                  m_gusc if s == 0 and J == 0 else None])
            m_chi = S.op("activation", c4[:, 512:1024], pk, AF.Identity,
                         scale=0.0625, bias=-0.499999,
                         waits=[pm.get("xb"), pkA_marks[(mat, dbuf)]])
            m_clo = V.op("scalar_tensor_tensor", c4[:, 0:512], c4[:, 512:1024],
                         -16.0, pk, ALU.mult, ALU.add, noserial=True,
                         waits=[m_chi, pm.get("xb")])
            m_s15 = V.op("tensor_scalar", s15, c4, 3, 15,
                         ALU.logical_shift_right, ALU.logical_shift_left,
                         noserial=True, waits=[m_clo, pm.get("orr")])
            m_and = V.op("tensor_single_scalar", c4, c4, 7, ALU.bitwise_and,
                         noserial=True, waits=[m_s15])
            xb_i = gubuf_i[mat][dbuf][:, 1024 * J:1024 * (J + 1)]
            wl = [m_and] + _aslist(fenceA.get(s - 2, m_p5_te))
            m_xb = V.op("tensor_scalar", xb_i, c4, 64, 16128,
                        ALU.mult, ALU.add, noserial=True, waits=wl)
            pm.update(sc=m_sc, chi=m_chi, clo=m_clo, s15=m_s15, xb=m_xb)

        def ffnA_s2(s, u):
            mat, J = u % 2, u // 2
            pm = pmA[mat]
            dbuf = s % 2
            r, s15 = rA[mat], s15A[mat]
            scb2 = scb2A[2 * mat + J % 2]
            xb_i = gubuf_i[mat][dbuf][:, 1024 * J:1024 * (J + 1)]
            xb_bf = gubuf[mat][dbuf][:, 1024 * J:1024 * (J + 1)]
            m_r = S.op("activation", r, xb_i, AF.Relu, bias=63.5,
                       scale=-1.0 / 256.0, waits=[pm["xb"], pm.get("val")])
            m_val = V.op("tensor_tensor", xb_bf, xb_bf, r, ALU.subtract,
                         noserial=True, waits=[m_r])
            m_or = V.op("tensor_tensor", xb_i, xb_i, s15, ALU.add,
                        noserial=True, waits=[m_val])
            m_m0 = V.op("tensor_tensor", xb_bf[:, 0:512], xb_bf[:, 0:512],
                        scb2, ALU.mult, noserial=True,
                        waits=[m_or, pm["sc"]])
            m_m1 = V.op("tensor_tensor", xb_bf[:, 512:1024],
                        xb_bf[:, 512:1024], scb2, ALU.mult, noserial=True,
                        waits=[m_m0])
            pm.update(r=m_r, val=m_val, orr=m_or, mul1=m_m1)
            scparA[mat][J % 2] = m_m1
            guready.setdefault(s, []).append(m_m1)

        unitsA = []
        dqA = {"i1": 0, "i2": 0}

        def emitA_unit():
            if dqA["i1"] < len(unitsA):
                ffnA_s1(*unitsA[dqA["i1"]])
                dqA["i1"] += 1
            lim = dqA["i1"] - (1 if dqA["i1"] < len(unitsA) else 0)
            if dqA["i2"] < lim:
                ffnA_s2(*unitsA[dqA["i2"]])
                dqA["i2"] += 1

        def emitA_chunk(s, ch):
            dbuf = s % 2
            par = ch % 2
            C = 4 * s + ch
            if ch == 0:
                for mk in guready[s]:
                    TE.wait(mk)
                TE.wait(m_xnq)
            for mk in _aslist(psA_war[par]):
                TE.wait(mk)
            m_g = m_u2 = None
            for mat in range(2):
                for th in range(2):
                    for g in range(16):
                        kw = dict(start=(g == 0), stop=(g == 15))
                        off = 1024 * (g // 2) + 512 * (g % 2) + 128 * ch
                        lhsT = gubuf[mat][dbuf][:, off:off + 128]
                        rhs = xnqA[:, NQ * g + 512 * th:NQ * g + 512 * (th + 1)]
                        if par == 0:
                            dst = [ps_ab, ps_cd][mat][:, 512 * th:512 * (th + 1)]
                        else:
                            dst = ps_big[:, 1024 * mat + 512 * th:
                                         1024 * mat + 512 * (th + 1)]
                        if (th, g) == (1, 15):
                            mk = TE.mile("matmul", dst, lhsT, rhs, **kw)
                            if mat == 0:
                                m_g = mk
                            else:
                                m_u2 = mk
                        else:
                            TE.mm(dst, lhsT, rhs, **kw)
            pgf = ps_ab[:, 0:1024] if par == 0 else ps_big[:, 0:1024]
            puf = ps_cd[:, 0:1024] if par == 0 else ps_big[:, 1024:2048]
            m_sq2 = S.op("activation", glA[0], pgf, AF.Square, scale=SQ_GC3,
                         waits=[m_g] + _aslist(sqA_war[0]))
            m_v3 = V.op("scalar_tensor_tensor", glA[2], glA[0], GC1, pgf,
                        ALU.add, ALU.mult, waits=[m_sq2])
            m_t = S.op("activation", glA[3], glA[2], AF.Tanh, waits=[m_v3])
            m_up = S.op("activation", glA[0], puf, AF.Identity, scale=0.5,
                        waits=[m_u2, m_v3])
            m_pm = V.op("tensor_tensor", glA[1], pgf, glA[0], ALU.mult,
                        waits=[m_up])
            m_int = V.op("scalar_tensor_tensor", ievA[ch % 2], glA[3], 1.0,
                         glA[1], ALU.add, ALU.mult,
                         waits=[m_t, ievA_war[ch % 2]])
            m_isp = Dsy.dma(f"sp_int{ch % 2}", inter_d.ap()[C], ievA[ch % 2],
                            waits=[m_int])
            ispill[C] = m_isp
            ievA_war[ch % 2] = m_isp
            psA_war[par] = [m_pm, m_up]
            sqA_war[0] = m_pm

        seedA = [mb_v, mb_s] + m_spwd
        pkA_war = [[seedA, seedA], [seedA, seedA]]
        for it in range(33):
            s = it
            prev = it - 1
            if s < 32:
                dbuf = s % 2
                for mat in range(2):
                    pkA_marks[(mat, dbuf)] = Dsy.dma(
                        f"ld_pkA{mat}{dbuf}", pkA[mat][dbuf],
                        [gate_pk3, up_pk3][mat].ap()[s],
                        waits=_aslist(pkA_war[mat][dbuf]))
                unitsA.extend((s, u) for u in range(16))
            for step in range(4):
                for _ in range(4):
                    emitA_unit()
                if prev >= 0:
                    emitA_chunk(prev, step)
            if s < 32:
                for mat in range(2):
                    pkA_war[mat][s % 2] = [pmA[mat]["chi"], pmA[mat]["clo"]]
            if prev >= 0:
                fenceA[prev] = TE.fence()
        while dqA["i2"] < len(unitsA):
            emitA_unit()
        mA_v = V.mark()
        mA_s = S.mark()
        m_scA_done = [p.mark(f"ld_scA{m2}{pq}")
                      for m2 in range(2) for pq in range(2)]

        wdbB = [v8(AR1, 8192 * i, 8192, BF16, 4096) for i in range(4)]
        ibufB = [v8(AR1, 32768 + 4096 * i, 4096, BF16, 2048) for i in range(4)]
        finxB = [v8(AR0, 0, 4096, F32, 1024), v8(AR0, 4096, 4096, F32, 1024)]
        finsB = [v8(AR0, 8192, 4096, F32, 1024),
                 v8(AR0, 12288, 4096, F32, 1024)]
        finlB = [v8(AR0, 16384, 4096, F32, 1024),
                 v8(AR0, 20480, 4096, F32, 1024)]

        fenceB = {}
        m_arB = ([mA_v, mA_s] + m_scD + _aslist(fenceA[31])
                 + _aslist(fenceA.get(30)))
        psq_war = [m_arB] * 4
        fin_war = [[None, None], [None, None], [None, None]]
        for pi in range(4):
            hh, tg = pi // 2, pi % 2
            m_blast = None
            for b in range(32):
                bg = 32 * pi + b
                dbuf = bg % 4
                wl = wdsp[(b, hh)] + _aslist(fenceB.get(bg - 4))
                if pi == 0 and b < 4:
                    wl += m_arB
                m_wd = Dsy.dma(f"ld_wdB{dbuf}", wdbB[dbuf],
                               wdT_dram.ap()[b, :, 4096 * hh:4096 * (hh + 1)],
                               waits=wl)
                ibm = []
                for g in range(4):
                    wl = [ispill[4 * b + g]] if pi == 0 else []
                    if g == 0:
                        wl = wl + _aslist(fenceB.get(bg - 4))
                        if pi == 0 and b < 4:
                            wl = wl + [mA_v]
                    ibm.append(Dsc.dma(
                        f"ld_intB{dbuf}",
                        ibufB[dbuf][:, 512 * g:512 * (g + 1)],
                        inter_d.ap()[4 * b + g, :, 512 * tg:512 * (tg + 1)],
                        waits=wl))
                TE.wait(m_wd)
                for mk in ibm:
                    TE.wait(mk)
                if b == 0:
                    for mk in _aslist(psq_war):
                        TE.wait(mk)
                for tc4 in range(4):
                    for g in range(4):
                        kw = dict(start=(b == 0 and g == 0),
                                  stop=(b == 31 and g == 3))
                        lhsT = ibufB[dbuf][:, 512 * g + 128 * tc4:
                                           512 * g + 128 * (tc4 + 1)]
                        for hq in range(2):
                            rhs = wdbB[dbuf][:, 1024 * g + 512 * hq:
                                             1024 * g + 512 * (hq + 1)]
                            if tc4 < 2:
                                dst = [ps_ab, ps_cd][tc4][:, 512 * hq:
                                                          512 * (hq + 1)]
                            else:
                                dst = ps_big[:, 1024 * (tc4 - 2) + 512 * hq:
                                             1024 * (tc4 - 2) + 512 * (hq + 1)]
                            if (tc4, g, hq) == (3, 3, 1):
                                m_blast = TE.mile("matmul", dst, lhsT, rhs,
                                                  **kw)
                            else:
                                TE.mm(dst, lhsT, rhs, **kw)
                fenceB[bg] = TE.fence()
            nw = []
            for tc4 in range(4):
                tc = 4 * tg + tc4
                fp = tc4 % 2
                psq = [ps_ab[:, 0:1024], ps_cd[:, 0:1024],
                       ps_big[:, 0:1024], ps_big[:, 1024:2048]][tc4]
                wl = [fin_war[0][fp]]
                if pi == 0 and tc4 < 2:
                    wl += m_arB
                m_fx = Dsy.dma(f"ld_fx{fp}", finxB[fp],
                               x_q.ap()[128 * tc:128 * (tc + 1),
                                        1024 * hh:1024 * (hh + 1)], waits=wl)
                m_fs = Dsy.dma(f"ld_fs{fp}", finsB[fp],
                               s_d.ap()[128 * tc:128 * (tc + 1),
                                        1024 * hh:1024 * (hh + 1)],
                               waits=[m_sp_s if (pi == 0 and tc4 == 0) else None,
                                      fin_war[1][fp]] +
                                     (m_arB if pi == 0 and tc4 < 2 else []))
                wl = [m_blast, fin_war[2][fp]]
                if pi == 0 and tc4 < 2:
                    wl += m_arB
                m_sil = S.op("activation", finlB[fp], psq, AF.Silu, waits=wl)
                m_o1 = V.op("tensor_tensor", finlB[fp], finlB[fp], finsB[fp],
                            ALU.mult, waits=[m_sil, m_fs])
                m_o2 = V.op("tensor_tensor", finlB[fp], finlB[fp], finxB[fp],
                            ALU.add, waits=[m_fx])
                m_out = Dsc.dma("sp_out",
                                out.ap()[128 * tc:128 * (tc + 1),
                                         1024 * hh:1024 * (hh + 1)],
                                finlB[fp], waits=[m_o2])
                fin_war[0][fp] = m_o2
                fin_war[1][fp] = m_o1
                fin_war[2][fp] = [m_o1, m_out]
                nw.append(m_sil)
            psq_war = nw

        p.wait("sync", p.mark("sp_out"))

        with nc.Block() as block:
            @block.tensor
            def _(eng):
                for f in p.body["tensor"]:
                    f(eng)

            @block.vector
            def _(eng):
                for f in p.body["vector"]:
                    f(eng)

            @block.scalar
            def _(eng):
                for f in p.body["scalar"]:
                    f(eng)

            @block.gpsimd
            def _(eng):
                for f in p.body["gpsimd"]:
                    f(eng)

            @block.sync
            def _(eng):
                for f in p.body["sync"]:
                    f(eng)

        res = run_bass_kernel_spmd(nc, in_maps, core_ids=list(range(N_CORES)),
                                   trace=trace)
    return res


def kernel(**inputs):
    in_maps = _host_inputs(inputs)
    res = build_and_run(in_maps)
    full = np.empty((2, T, D), np.float32)
    for c in range(N_CORES):
        b, q = divmod(c, 4)
        oc = np.asarray(res.results[c]["out"])
        for jt, t in enumerate(TILES[q]):
            full[b, 256 * t:256 * (t + 1)] = oc[256 * jt:256 * (jt + 1)]
    return full


# version 21
# speedup vs baseline: 1.1305x; 1.0240x over previous
import math
import numpy as np
import concourse.bass as bass
import concourse.mybir as mybir
from concourse.bass_utils import run_bass_kernel_spmd

F32 = mybir.dt.float32
BF16 = mybir.dt.bfloat16
U8 = mybir.dt.uint8
I16 = mybir.dt.int16
AF = mybir.ActivationFunctionType
ALU = mybir.AluOpType
AX = mybir.AxisListType

D = 2048
T = 4096
NQ = 1024
NH = 16
VD = 128
NF = 16384
LAMBDA_INIT = 0.8 - 0.6 * math.exp(-0.3)
ONE_M_LI = 1.0 - LAMBDA_INIT
GC1 = 0.7978845608028654
GC3 = 0.7978845608028654 * 0.044715
EPS = 1e-5

SQ_GC3 = math.sqrt(GC3)
CONST_VALS = [0.0, 1.0, 0.125, -12.0, 0.5, -0.25, EPS, 1.0 / D, 1.0 / VD,
              -0.5, math.log(ONE_M_LI), GC1, GC3, LAMBDA_INIT, 0.0625,
              -0.499999, 63.5, -1.0 / 256.0, SQ_GC3]

N_CORES = 8
TILES = [[c4, 4 + c4, 11 - c4, 15 - c4] for c4 in range(4)]
PC = [8, 16, 24, 32]
MOFF = [0, 8, 24, 48]
NCH_TOT = 80



def _nperm():
    p = []
    for b in range(NF // 256):
        p.extend(range(256 * b, 256 * (b + 1), 2))
        p.extend(range(256 * b + 1, 256 * (b + 1), 2))
    return np.array(p)


def _pack_u8(a):
    a = np.ascontiguousarray(np.asarray(a, np.int32))
    return np.ascontiguousarray(a.view(np.uint8).reshape(*a.shape, 4)[..., 0])


def _chunkmajor(w):
    Dr, C = w.shape
    return np.ascontiguousarray(w.reshape(Dr // 128, 128, C).transpose(1, 0, 2)
                                .reshape(128, (Dr // 128) * C))


def _host_inputs(inputs):
    x = np.asarray(inputs["L0_attn_x"], np.float32).reshape(2, T, D)
    wqkv = np.asarray(inputs["L0_attn_Wqkv"], np.float32)
    wproj = np.asarray(inputs["L0_attn_Wproj"], np.float32)

    kcols, vcols, qcols = [], [], []
    for h in range(NH):
        kcols.extend(range(1024 + 64 * h, 1024 + 64 * h + 64))
        kcols.extend(range(3072 + 64 * h, 3072 + 64 * h + 64))
    for h in range(NH):
        vcols.extend(range(4096 + 128 * h, 4096 + 128 * h + 128))
    for h in range(NH):
        qcols.extend(range(64 * h, 64 * h + 64))
        qcols.extend(range(2048 + 64 * h, 2048 + 64 * h + 64))
    wqkv_k = _chunkmajor(np.ascontiguousarray(wqkv[:, kcols]))
    wqkv_v = _chunkmajor(np.ascontiguousarray(wqkv[:, vcols]))
    wqkv_q = _chunkmajor(np.ascontiguousarray(wqkv[:, qcols]))
    wproj_cm = _chunkmajor(np.ascontiguousarray(wproj))

    lamv = np.zeros((128, 256), np.float32)
    lamv[:, 0:64] = np.asarray(inputs["L0_attn_lambda_q1"], np.float32)
    lamv[:, 64:128] = np.asarray(inputs["L0_attn_lambda_q2"], np.float32)
    lamv[:, 128:192] = np.asarray(inputs["L0_attn_lambda_k1"], np.float32)
    lamv[:, 192:256] = np.asarray(inputs["L0_attn_lambda_k2"], np.float32)

    import ml_dtypes
    consts = np.zeros((128, 20), np.float32)
    consts[:, :len(CONST_VALS)] = CONST_VALS
    ident = np.eye(128, dtype=ml_dtypes.bfloat16)

    perm = _nperm()

    def _gu_blocks(pk_t):
        a = pk_t.reshape(8, 128, 32, 512)
        return np.ascontiguousarray(a.transpose(2, 1, 0, 3).reshape(32, 128, 4096))

    gate_pk3 = _gu_blocks(_pack_u8(inputs["L0_ffn_gate_W_packed"])[perm, :].T)
    up_pk3 = _gu_blocks(_pack_u8(inputs["L0_ffn_up_W_packed"])[perm, :].T)
    d8 = _pack_u8(inputs["L0_ffn_down_W_packed"]).T
    down_pk3 = np.ascontiguousarray(
        d8.reshape(32, 2, 128, 2, 1024).transpose(0, 2, 1, 3, 4)
        .reshape(32, 128, 4096))
    gsc_t = np.asarray(inputs["L0_ffn_gate_scales"], np.float32)[perm, :].T
    usc_t = np.asarray(inputs["L0_ffn_up_scales"], np.float32)[perm, :].T
    gu_sc = np.ascontiguousarray(np.concatenate([gsc_t, usc_t], 0))
    dsc_t = np.asarray(inputs["L0_ffn_down_scales"], np.float32).T
    ds_sc = np.ascontiguousarray(
        dsc_t.reshape(4, 128, 2048).transpose(1, 0, 2).reshape(128, 8192))

    shared = {
        "wqkv_k": wqkv_k, "wqkv_v": wqkv_v, "wqkv_q": wqkv_q, "wproj": wproj_cm,
        "lamv": lamv, "consts": consts, "ident": ident,
        "gate_pk3": gate_pk3, "up_pk3": up_pk3, "down_pk3": down_pk3,
        "gu_sc": gu_sc, "ds_sc": ds_sc,
    }

    kt = np.arange(128)
    qt = np.arange(256)
    in_maps = []
    for c in range(N_CORES):
        b, c4 = divmod(c, 4)
        tiles = TILES[c4]
        m = np.zeros((32, 128, 512), ml_dtypes.bfloat16)
        for jt in range(4):
            for i8 in range(8):
                kc = PC[jt] - 8 + i8
                m1 = ((128 * kc + kt[:, None])
                      <= (256 * tiles[jt] + qt[None, :256])).astype(np.float32)
                m[jt * 8 + i8] = np.concatenate([m1, m1], 1)
        d = dict(shared)
        d["x_kv"] = np.ascontiguousarray(x[b])
        d["x_q"] = np.ascontiguousarray(
            np.concatenate([x[b, 256 * t:256 * (t + 1)] for t in tiles], 0))
        d["masks"] = np.ascontiguousarray(m.transpose(1, 0, 2).reshape(128, 32 * 512))
        in_maps.append(d)
    return in_maps



def _aslist(w):
    if w is None:
        return []
    if isinstance(w, tuple) and len(w) == 2 and isinstance(w[0], str):
        return [w]
    out = []
    for m in w:
        out.extend(_aslist(m))
    return out


class Prog:
    ENGINES = ("tensor", "vector", "scalar", "gpsimd", "sync")

    def __init__(self, nc, ctx):
        self.nc = nc
        self.ctx = ctx
        self.body = {e: [] for e in self.ENGINES}
        self.sems = {}
        self.waited = {e: {} for e in self.ENGINES}

    def sem(self, name):
        if name not in self.sems:
            h = self.ctx.enter_context(self.nc.semaphore(name))
            self.sems[name] = [h, 0]
        return self.sems[name]

    def wait(self, eng, mark):
        for m in _aslist(mark):
            if m[1] <= 0 or self.waited[eng].get(m[0], 0) >= m[1]:
                continue
            self.waited[eng][m[0]] = m[1]
            h = self.sems[m[0]][0]
            self.body[eng].append(lambda e, h=h, v=m[1]: e.wait_ge(h, v))

    def op(self, eng, fname, *a, inc=None, **k):
        if inc is not None:
            s = self.sem(inc[0])
            s[1] += inc[1]
            h, val = s[0], s[1]

            def run(e, fname=fname, a=a, k=k, h=h, amt=inc[1]):
                getattr(e, fname)(*a, **k).then_inc(h, amt)
            self.body[eng].append(run)
            return (inc[0], val)

        def run(e, fname=fname, a=a, k=k):
            getattr(e, fname)(*a, **k)
        self.body[eng].append(run)
        return None

    def mark(self, name):
        return (name, self.sems[name][1]) if name in self.sems else (name, 0)


SELF_SERIAL = True


class SerialEngine:
    def __init__(self, prog, eng, semname, serial=True):
        self.p, self.eng, self.semname = prog, eng, semname
        self.serial = serial
        self.p.sem(semname)

    def __init2__(self):
        pass

    def op(self, fname, *a, waits=(), noserial=False, **k):
        self.p.wait(self.eng, waits)
        if self.serial and not noserial:
            n = self.p.sems[self.semname][1]
            if n > 0:
                self.p.wait(self.eng, (self.semname, n))
        return self.p.op(self.eng, fname, *a, inc=(self.semname, 1), **k)

    def mark(self):
        return self.p.mark(self.semname)


class DmaQueue:
    def __init__(self, prog, eng):
        self.p, self.eng = prog, eng

    def dma(self, semname, out, in_, waits=(), transpose=False):
        self.p.wait(self.eng, waits)
        n = self.p.sems[semname][1] if semname in self.p.sems else 0
        if n > 0:
            self.p.wait(self.eng, (semname, n))
        if transpose:
            return self.p.op(self.eng, "dma_start_transpose", out, in_,
                             inc=(semname, 16))
        return self.p.op(self.eng, "dma_start", out=out, in_=in_,
                         inc=(semname, 16))

    def dma_group(self, semname, pairs, waits=()):
        self.p.wait(self.eng, waits)
        for out, in_ in pairs:
            self.p.op(self.eng, "dma_start", out=out, in_=in_, inc=(semname, 16))
        return self.p.mark(semname)


class TensorEng:
    def __init__(self, prog):
        self.p = prog
        self.rot = 0
        self.p.sem("st0")
        self.p.sem("st1")

    def wait(self, mark):
        self.p.wait("tensor", mark)

    def mm(self, *a, **k):
        self.p.op("tensor", "matmul", *a, **k)

    def mile(self, fname, *a, **k):
        name = f"st{self.rot}"
        self.rot ^= 1
        n = self.p.sems[name][1]
        if n > 0:
            self.p.wait("tensor", (name, n))
        return self.p.op("tensor", fname, *a, inc=(name, 1), **k)

    def fence(self):
        return [self.p.mark("st0"), self.p.mark("st1")]



def build_and_run(in_maps, trace=False):
    from contextlib import ExitStack
    nc = bass.Bass("TRN2", target_bir_lowering=False)

    di = lambda n, s, dt: nc.dram_tensor(n, s, dt, kind="ExternalInput")
    dd = lambda n, s, dt: nc.dram_tensor(n, s, dt, kind="Internal")
    x_kv = di("x_kv", [T, D], F32)
    x_q = di("x_q", [NQ, D], F32)
    wqkv_k = di("wqkv_k", [128, 16 * 2048], F32)
    wqkv_v = di("wqkv_v", [128, 16 * 2048], F32)
    wqkv_q = di("wqkv_q", [128, 16 * 2048], F32)
    wproj = di("wproj", [128, 16 * 2048], F32)
    lamv = di("lamv", [128, 256], F32)
    consts = di("consts", [128, 20], F32)
    ident = di("ident", [128, 128], BF16)
    masks = di("masks", [128, 32 * 512], BF16)
    gate_pk3 = di("gate_pk3", [32, 128, 4096], U8)
    up_pk3 = di("up_pk3", [32, 128, 4096], U8)
    down_pk3 = di("down_pk3", [32, 128, 4096], U8)
    gu_sc = di("gu_sc", [128, NF], F32)
    ds_sc = di("ds_sc", [128, 8192], F32)
    out = nc.dram_tensor("out", [NQ, D], F32, kind="ExternalOutput")

    xn_kv_d = dd("xn_kv_d", [T, D], BF16)
    xn_q_d = dd("xn_q_d", [NQ, D], BF16)
    xnqT_d = dd("xnqT_d", [D, NQ], BF16)
    k12T_d = dd("k12T_d", [NH, 128, T], BF16)
    vext_d = dd("vext_d", [NH, 128, 32, 132], BF16)
    q12T_d = dd("q12T_d", [NH, 128, NQ], BF16)
    s_d = dd("s_d", [NQ, D], F32)
    attnT_d = dd("attnT_d", [8, 128, NH, 128], BF16)
    inter_d = dd("inter_d", [128, 128, NQ], BF16)
    wdT_dram = dd("wdT_dram", [32, 128, 8192], BF16)

    ctx = ExitStack()
    with ctx:
        sb = lambda n, s, dt: ctx.enter_context(nc.sbuf_tensor(n, s, dt))
        psal = lambda n, s, dt: ctx.enter_context(nc.psum_tensor(n, s, dt))

        AR0 = sb("AR0", [128, 65536], U8)
        AR1 = sb("AR1", [128, 65536], U8)
        AR2 = sb("AR2", [128, 49152], U8)
        AR3 = sb("AR3", [128, 22528], U8)

        def v8(ar, off, nbytes, dt, cols):
            return ar[:, off:off + nbytes].bitcast(dt)[:, 0:cols]

        wk_sb = v8(AR0, 0, 65536, BF16, 16 * 2048)
        wv_sb = wk_sb
        wq_sb = wk_sb
        wp_sb = wk_sb

        xt = [v8(AR1, 0, 8192, F32, D), v8(AR1, 8192, 8192, F32, D)]
        lnsq = v8(AR1, 16384, 8192, F32, D)
        xnt = [v8(AR1, 24576, 4096, BF16, D), v8(AR1, 28672, 4096, BF16, D)]
        xnT_sb = [v8(AR2, 0, 16384, BF16, 16 * 512),
                  v8(AR2, 16384, 16384, BF16, 16 * 512)]
        qkv_ev = [v8(AR1, 0, 1024, BF16, 512), v8(AR1, 1024, 1024, BF16, 512)]
        vtr_ev = [v8(AR1, 2048, 1024, BF16, 512), v8(AR1, 3072, 1024, BF16, 512)]
        vex_ev = [v8(AR1, 4096, 1088, BF16, 4 * 132), v8(AR1, 5248, 1088, BF16, 4 * 132)]

        k12 = [v8(AR2, 0, 8192, BF16, T), v8(AR2, 8192, 8192, BF16, T)]
        vex = [v8(AR2, 16384, 8448, BF16, 32 * 132), v8(AR2, 24832, 8448, BF16, 32 * 132)]
        q12 = [v8(AR2, 33280, 2048, BF16, NQ), v8(AR2, 35328, 2048, BF16, NQ)]
        att_qs = [v8(AR2, 0, 4096, BF16, D), v8(AR2, 4096, 4096, BF16, D)]
        s_ev = [v8(AR2, 8192, 8192, F32, D)]

        cst = sb("cst", [128, 20], F32)
        for i, v in enumerate(CONST_VALS):
            nc.const_aps.register(F32, v, cst[:, i:i + 1])
        lam_sb = sb("lam_sb", [128, 8], F32)
        lamv_sb = v8(AR1, 16384, 1024, F32, 256)
        lamx = v8(AR1, 20480, 1024, F32, 256)
        id_b = sb("id_b", [128, 128], BF16)
        lnst = sb("lnst", [128, 8], F32)
        ep_st = sb("ep_st", [128, 16], F32)
        p12 = [sb(f"p12_{i}", [128, 1024], BF16) for i in range(2)]
        scr = [sb(f"scr{i}", [128, 256], F32) for i in range(4)]
        ep_t2 = scr[0][:, 0:128]
        ep_diff = scr[1][:, 0:128]
        ep_sq = scr[2][:, 0:128]
        ep_on = scr[3][:, 0:128].bitcast(BF16)[:, 0:128]
        gl_sq, gl_v2, gl_v3, gl_t = (s[:] for s in scr)
        gl_up = scr[2][:]
        att_st = [sb(f"att_st{i}", [128, 128], BF16) for i in range(2)]

        ps_ab = psal("ps_ab", [128, 1024], F32)
        ps_cd = psal("ps_cd", [128, 1024], F32)
        ps_big = psal("ps_big", [128, 2048], F32)

        class _PV:
            def __init__(self, t, lo):
                self.t, self.lo = t, lo
            def __getitem__(self, key):
                assert isinstance(key, tuple) or key == slice(None)
                if not isinstance(key, tuple):
                    return self.t[:, self.lo:self.lo + 512]
                p, f = key
                assert p == slice(None)
                return self.t[:, self.lo + (f.start or 0):self.lo + f.stop]
        ps_a = _PV(ps_ab, 0)
        ps_b = _PV(ps_ab, 512)
        ps_c = _PV(ps_cd, 0)
        ps_d = _PV(ps_cd, 512)
        ps4 = [ps_a, ps_b, ps_c, ps_d]

        p = Prog(nc, ctx)
        V = SerialEngine(p, "vector", "sv")
        S = SerialEngine(p, "scalar", "ss", serial=False)
        G = SerialEngine(p, "gpsimd", "sg", serial=False)
        TE = TensorEng(p)
        Dsy = DmaQueue(p, "sync")
        Dsc = DmaQueue(p, "scalar")
        Dgp = DmaQueue(p, "gpsimd")

        m_ini = Dsy.dma_group("ld_ini", [
            (cst[:], consts.ap()),
            (id_b[:], ident.ap()),
            (lamv_sb, lamv.ap()),
        ])
        m_wk = Dgp.dma("ld_w0", wk_sb, wqkv_k.ap())
        ds_sc_sb = v8(AR1, 32768, 16384, BF16, 8192)
        m_dssc = Dgp.dma("ld_dssc", ds_sc_sb, ds_sc.ap())

        S.op("copy", lam_sb[:, 1:2], cst[:, 0:1], waits=[m_ini])

        V.op("tensor_tensor", lamx[:, 0:128], lamv_sb[:, 0:128],
             lamv_sb[:, 128:256], ALU.mult, waits=[m_ini])
        V.op("reduce_sum", lam_sb[:, 2:3], lamx[:, 0:64], axis=AX.X)
        m_d12 = V.op("reduce_sum", lam_sb[:, 3:4], lamx[:, 64:128], axis=AX.X)
        S.op("activation", lam_sb[:, 4:5], lam_sb[:, 2:3], AF.Exp,
             waits=[m_d12])
        m_e2 = S.op("activation", lam_sb[:, 5:6], lam_sb[:, 3:4], AF.Exp)
        V.op("tensor_tensor", lam_sb[:, 6:7], lam_sb[:, 4:5],
             lam_sb[:, 5:6], ALU.subtract, waits=[m_e2])
        m_lam = V.op("tensor_scalar", lam_sb[:, 0:1], lam_sb[:, 6:7],
                     LAMBDA_INIT, None, ALU.add)

        sp_xn_mark = [None, None]
        ln_v_mark = [None, None]
        for i in range(40):
            par = i % 2
            if i < 32:
                src_ = x_kv.ap()[128 * i:128 * (i + 1), :]
                dst = xn_kv_d.ap()[128 * i:128 * (i + 1), :]
            else:
                j = i - 32
                src_ = x_q.ap()[128 * j:128 * (j + 1), :]
                dst = xn_q_d.ap()[128 * j:128 * (j + 1), :]
            m_ld = Dsy.dma(f"ld_x{par}", xt[par], src_,
                           waits=[ln_v_mark[par], m_lam if i == 0 else None])
            V.op("reduce_sum", lnst[:, 0:1], xt[par], axis=AX.X,
                 waits=[m_ld])
            m_sq = S.op("activation", lnsq, xt[par], AF.Square,
                        accum_out=lnst[:, 1:2], waits=[m_ld, V.mark()])
            V.op("tensor_scalar", lnst[:, 2:3], lnst[:, 0:1], 1.0 / D, None,
                 ALU.mult, waits=[m_sq])
            V.op("tensor_tensor", lnst[:, 3:4], lnst[:, 2:3], lnst[:, 2:3],
                 ALU.mult)
            V.op("tensor_scalar", lnst[:, 4:5], lnst[:, 1:2], 1.0 / D, EPS,
                 ALU.mult, ALU.add)
            m_var = V.op("tensor_tensor", lnst[:, 5:6], lnst[:, 4:5],
                         lnst[:, 3:4], ALU.subtract)
            S.op("activation", lnst[:, 6:7], lnst[:, 5:6], AF.Ln,
                 waits=[m_var])
            m_rstd = S.op("activation", lnst[:, 7:8], lnst[:, 6:7], AF.Exp,
                          scale=-0.5)
            m_xn = V.op("tensor_scalar", xnt[par], xt[par],
                        lnst[:, 2:3], lnst[:, 7:8], ALU.subtract, ALU.mult,
                        waits=[m_rstd, sp_xn_mark[par]])
            ln_v_mark[par] = m_xn
            sp_xn_mark[par] = Dsc.dma(f"sp_xn{par}", dst, xnt[par],
                                      waits=[m_xn])
        m_ln_done = [sp_xn_mark[0], sp_xn_mark[1]]
        m_ln_v = V.mark()

        adl = [dict(c4=v8(AR1, 49152, 4096, I16, 2048),
                    s15=v8(AR1, 53248, 4096, I16, 2048),
                    xb=v8(AR1, 57344, 4096, I16, 2048),
                    xbf=v8(AR1, 57344, 4096, BF16, 2048),
                    r=v8(AR1, 61440, 4096, BF16, 2048),
                    pk=v8(AR3, 0, 1024, U8, 1024),
                    sc=v8(AR3, 1024, 2048, BF16, 1024)),
               dict(c4=v8(AR3, 3072, 4096, I16, 2048),
                    s15=v8(AR3, 7168, 4096, I16, 2048),
                    xb=v8(AR3, 11264, 4096, I16, 2048),
                    xbf=v8(AR3, 11264, 4096, BF16, 2048),
                    r=v8(AR3, 15360, 4096, BF16, 2048),
                    pk=v8(AR3, 19456, 1024, U8, 1024),
                    sc=v8(AR3, 20480, 2048, BF16, 1024))]
        ad_pm = [dict(), dict()]
        wdsp = {}
        ad_state = {"i1": 0, "i2": 0}

        def adq_s1(u):
            Jp, cc = u // 2, u % 2
            L = u % 2
            t = adl[L]
            pm = ad_pm[L]
            b, j = Jp // 2, Jp % 2
            m_pk = Dsc.dma(f"ld_pkD{L}", t["pk"],
                           down_pk3.ap()[b, :, 1024 * (2 * j + cc):
                                         1024 * (2 * j + cc + 1)],
                           waits=[pm.get("chi"), pm.get("clo")])
            p0 = (8 * Jp) % 128
            c0 = 2048 * (Jp // 16) + 1024 * cc
            src_sc = (ds_sc_sb[p0:p0 + 8, c0:c0 + 1024]
                      .unsqueeze(1).broadcast_to([8, 16, 1024]))
            m_sc = Dsc.dma(f"ld_scD{L}", t["sc"], src_sc,
                           waits=[pm.get("mul1"),
                                  m_dssc if u < 2 else None])
            m_chi = S.op("activation", t["c4"][:, 1024:2048], t["pk"],
                         AF.Identity, scale=0.0625, bias=-0.499999,
                         waits=[m_pk, pm.get("xb")])
            m_clo = V.op("scalar_tensor_tensor", t["c4"][:, 0:1024],
                         t["c4"][:, 1024:2048], -16.0, t["pk"],
                         ALU.mult, ALU.add, noserial=True,
                         waits=[m_chi, pm.get("xb")])
            m_s15 = V.op("tensor_scalar", t["s15"], t["c4"], 3, 15,
                         ALU.logical_shift_right, ALU.logical_shift_left,
                         noserial=True, waits=[m_clo, pm.get("orr")])
            m_and = V.op("tensor_single_scalar", t["c4"], t["c4"], 7,
                         ALU.bitwise_and, noserial=True, waits=[m_s15])
            m_xb = V.op("tensor_scalar", t["xb"], t["c4"], 64, 16128,
                        ALU.mult, ALU.add, noserial=True,
                        waits=[m_and] + _aslist(pm.get("sp")))
            pm.update(chi=m_chi, clo=m_clo, s15=m_s15, xb=m_xb, sc=m_sc)

        def adq_s2(u):
            Jp, cc = u // 2, u % 2
            L = u % 2
            t = adl[L]
            pm = ad_pm[L]
            b, j = Jp // 2, Jp % 2
            m_r = S.op("activation", t["r"], t["xb"], AF.Relu, bias=63.5,
                       scale=-1.0 / 256.0, waits=[pm["xb"], pm.get("val")])
            m_val = V.op("tensor_tensor", t["xbf"], t["xbf"], t["r"],
                         ALU.subtract, noserial=True, waits=[m_r])
            m_or = V.op("tensor_tensor", t["xb"], t["xb"], t["s15"],
                        ALU.add, noserial=True, waits=[m_val])
            m_m0 = V.op("tensor_tensor", t["xbf"][:, 0:1024],
                        t["xbf"][:, 0:1024], t["sc"], ALU.mult,
                        noserial=True, waits=[m_or, pm["sc"]])
            m_m1 = V.op("tensor_tensor", t["xbf"][:, 1024:2048],
                        t["xbf"][:, 1024:2048], t["sc"], ALU.mult,
                        noserial=True, waits=[m_m0])
            o0 = 4096 * cc + 2048 * j
            m_spa = Dsc.dma(f"sp_wd{L}a",
                            wdT_dram.ap()[b, :, o0:o0 + 1024],
                            t["xbf"][:, 0:1024], waits=[m_m0])
            m_spb = Dsc.dma(f"sp_wd{L}b",
                            wdT_dram.ap()[b, :, o0 + 1024:o0 + 2048],
                            t["xbf"][:, 1024:2048], waits=[m_m1])
            pm.update(r=m_r, val=m_val, orr=m_or, mul1=m_m1,
                      sp=[m_spa, m_spb])
            wdsp.setdefault((b, cc), []).extend([m_spa, m_spb])

        def emit_adq():
            if ad_state["i1"] < 128:
                adq_s1(ad_state["i1"])
                ad_state["i1"] += 1
            lim = ad_state["i1"] - (1 if ad_state["i1"] < 128 else 0)
            if ad_state["i2"] < lim:
                adq_s2(ad_state["i2"])
                ad_state["i2"] += 1

        qkv_ps = [ps_a, ps_c]
        ps_war = [None, None]
        tr_bank_war = [None, None]
        xnT_war = [None, None]
        kq_war = [None, None]
        vtr_war = [None, None]
        vexev_war = [None, None]
        k_sp = v_sp = q_sp = None
        m_vex_init = None

        def qkv_pass(tiles, xn_src, m_w, mode):
            nonlocal k_sp, v_sp, q_sp, m_vex_init
            for jj, j in enumerate(tiles):
                par = jj % 2
                t0 = (j - 8) * 512 if mode == "q" else j * 512
                wlist = list(m_ln_done) + _aslist(xnT_war[par])
                for c0 in range(16):
                    Dsy.dma(f"ld_xnT{par}",
                            xnT_sb[par][:, 512 * c0:512 * (c0 + 1)],
                            xn_src.ap()[t0:t0 + 512, 128 * c0:128 * (c0 + 1)],
                            waits=wlist if c0 == 0 else (), transpose=True)
                m_xnT = p.mark(f"ld_xnT{par}")
                if mode == "q":
                    for c0 in range(16):
                        Dsc.dma("sp_xnqT",
                                xnqT_d.ap()[128 * c0:128 * (c0 + 1), t0:t0 + 512],
                                xnT_sb[par][:, 512 * c0:512 * (c0 + 1)],
                                waits=[m_xnT] if c0 == 0 else ())
                TE.wait(m_xnT)
                TE.wait(m_w)
                for cc in range(16):
                    ppi = cc % 2
                    pp = qkv_ps[ppi]
                    TE.wait(ps_war[ppi])
                    m_mm = None
                    for hc in range(16):
                        kw = dict(start=(hc == 0), stop=(hc == 15))
                        lhsT = wk_sb[:, 2048 * hc + 128 * cc:
                                     2048 * hc + 128 * (cc + 1)]
                        rhs = xnT_sb[par][:, 512 * hc:512 * (hc + 1)]
                        if hc == 15:
                            m_mm = TE.mile("matmul", pp[:], lhsT, rhs, **kw)
                        else:
                            TE.mm(pp[:], lhsT, rhs, **kw)
                    bufp = cc % 2
                    if mode in ("k", "q"):
                        m_ev = S.op("copy", qkv_ev[bufp], pp[:],
                                    waits=[m_mm, kq_war[bufp],
                                           m_ln_v if (jj == 0 and cc < 2) else None])
                        ps_war[ppi] = m_ev
                        if mode == "q":
                            m_sp = Dsc.dma("sp_q", q12T_d.ap()[cc, :, t0:t0 + 512],
                                           qkv_ev[bufp], waits=[m_ev])
                            q_sp = m_sp
                        else:
                            m_sp = Dsc.dma("sp_k", k12T_d.ap()[cc, :, t0:t0 + 512],
                                           qkv_ev[bufp], waits=[m_ev])
                            k_sp = m_sp
                        kq_war[bufp] = m_sp
                    else:
                        h = cc
                        m_ev = S.op("copy", vtr_ev[bufp], pp[:],
                                    waits=[m_mm, vtr_war[bufp]])
                        ps_war[ppi] = m_ev
                        cp_marks = [None] * 4
                        for kk in range(4):
                            bank = [ps_b, ps_d][kk % 2]
                            dstps = bank[:, 64 * (kk // 2):
                                         64 * (kk // 2 + 1)].bitcast(BF16)
                            if kk == 0:
                                TE.wait(m_ev)
                                TE.wait(m_ini)
                            TE.wait(cp_marks[kk - 2] if kk >= 2
                                    else tr_bank_war[kk % 2])
                            m_tr = TE.mile("transpose", dstps,
                                           vtr_ev[bufp][:, 128 * kk:128 * (kk + 1)],
                                           id_b[:])
                            cp_marks[kk] = S.op(
                                "copy", vex_ev[bufp][:, 132 * kk:132 * kk + 128],
                                dstps,
                                waits=[m_tr,
                                       vexev_war[bufp] if kk == 0 else None,
                                       m_vex_init if kk == 0 else None])
                        tr_bank_war[0] = cp_marks[2]
                        tr_bank_war[1] = cp_marks[3]
                        vtr_war[bufp] = cp_marks[3]
                        m_sp = Dsc.dma(
                            "sp_v", vext_d.ap()[h, :, 4 * j:4 * (j + 1), :],
                            vex_ev[bufp].rearrange("p (a f) -> p a f", a=4),
                            waits=[cp_marks[3]])
                        vexev_war[bufp] = m_sp
                        v_sp = m_sp
                    if cc % 4 == 1:
                        emit_adq()
                xnT_war[par] = TE.fence()

        m_vex_init = None
        for i2 in range(2):
            V.op("memset", vex_ev[i2], 0.0, waits=[m_ln_v])
            for kk in range(4):
                m_vex_init = V.op("memset",
                                  vex_ev[i2][:, 132 * kk + 128:132 * kk + 129],
                                  1.0)
        qkv_pass(list(range(8)), xn_kv_d, m_wk, "k")
        m_wv = Dgp.dma("ld_w1", wk_sb, wqkv_v.ap(), waits=TE.fence())
        qkv_pass(list(range(8)), xn_kv_d, m_wv, "v")
        m_wq2 = Dgp.dma("ld_w2", wk_sb, wqkv_q.ap(), waits=TE.fence())
        qkv_pass([8, 9], xn_q_d, m_wq2, "q")
        m_wp = Dgp.dma("ld_w3", wk_sb, wproj.ap(), waits=TE.fence())

        def off(mat, sub):
            return (sub // 2 + 2 * mat) * 512 + (sub % 2) * 129

        m_qkv_done = [p.mark("sp_k"), p.mark("sp_v"), p.mark("sp_q"),
                      p.mark("sp_xnqT")]
        msk_sb = v8(AR1, 0, 32768, BF16, 32 * 512)
        m_msk = Dsy.dma("ld_msk", msk_sb, masks.ap(), waits=m_qkv_done)
        m_msk_last = None

        hd_war = [None, None]
        exp_war = {}
        av_war = {}
        ep_copies = []
        att_sp = [None, None]
        first_att = True
        OBASE = [0, 512, 1024, 1536]

        for h in range(NH):
            hb = h % 2
            m_k = Dsy.dma(f"ld_k{hb}", k12[hb], k12T_d.ap()[h],
                          waits=[k_sp, v_sp, q_sp] + _aslist(hd_war[hb]))
            m_v = Dsy.dma(f"ld_v{hb}",
                          vex[hb].rearrange("p (a f) -> p a f", a=32),
                          vext_d.ap()[h])
            m_q = Dsy.dma(f"ld_q{hb}", q12[hb], q12T_d.ap()[h])
            for jt in range(4):
                prev_ep = list(ep_copies)
                ep_copies = []
                msk_marks = {}
                last = None

                def emit_scores(kc):
                    sp = kc % 2
                    pst = [ps_ab, ps_cd][sp]
                    pbuf = p12[(kc % 4) // 2][:, 512 * sp:512 * (sp + 1)]
                    if kc == 0:
                        TE.wait(m_k)
                        TE.wait(m_q)
                        TE.wait(m_v)
                        for mk in prev_ep:
                            TE.wait(mk)
                    if first_att and kc < 2:
                        TE.wait(ps_war[sp])
                        TE.wait(tr_bank_war[0])
                        TE.wait(tr_bank_war[1])
                    TE.wait(exp_war.get(sp))
                    TE.mm(pst[:, 0:256], k12[hb][0:64, 128 * kc:128 * (kc + 1)],
                          q12[hb][0:64, 256 * jt:256 * (jt + 1)],
                          start=True, stop=True)
                    m_s2 = TE.mile("matmul", pst[:, 512:768],
                                   k12[hb][64:128, 128 * kc:128 * (kc + 1)],
                                   q12[hb][64:128, 256 * jt:256 * (jt + 1)],
                                   start=True, stop=True)
                    sview = pst[:].rearrange("p (b c) -> p b c", b=2)[:, :, 0:256]
                    m_exp = S.op("activation",
                                 pbuf.rearrange("p (b c) -> p b c", b=2),
                                 sview, AF.Exp, bias=-12.0, scale=0.125,
                                 waits=[m_s2, av_war.get(kc % 4)])
                    exp_war[sp] = m_exp
                    base = PC[jt] - 8
                    if kc >= base:
                        mslice = msk_sb[:, (jt * 8 + kc - base) * 512:
                                        (jt * 8 + kc - base + 1) * 512]
                        m2 = V.op("tensor_tensor", pbuf, pbuf, mslice,
                                  ALU.mult, noserial=True,
                                  waits=[m_exp, m_msk, av_war.get(kc % 4)])
                        msk_marks[kc] = m2
                    else:
                        msk_marks[kc] = m_exp

                def emit_av(kc):
                    nonlocal last
                    sp = kc % 2
                    pbuf = p12[(kc % 4) // 2][:, 512 * sp:512 * (sp + 1)]
                    TE.wait(msk_marks[kc])
                    for mat in range(2):
                        for sub in range(2):
                            o = OBASE[2 * mat + sub]
                            kw = dict(start=(kc == 0), stop=(kc == PC[jt] - 1))
                            lhsT = pbuf[:, 256 * mat + 128 * sub:
                                        256 * mat + 128 * (sub + 1)]
                            rhs = vex[hb][:, 132 * kc:132 * kc + 129]
                            if (mat, sub) == (1, 1):
                                last = TE.mile("matmul", ps_big[:, o:o + 129],
                                               lhsT, rhs, **kw)
                            else:
                                TE.mm(ps_big[:, o:o + 129], lhsT, rhs, **kw)
                    av_war[kc % 4] = last

                for kc in range(PC[jt] + 2):
                    if kc < PC[jt]:
                        emit_scores(kc)
                    if kc >= 2:
                        emit_av(kc - 2)
                m_msk_last = msk_marks[PC[jt] - 1]
                first_att = False
                m_av_done = last
                eps = []
                for sub in range(2):
                    c0 = 8 * sub
                    o1 = OBASE[sub]
                    o2 = OBASE[2 + sub]
                    t2v = scr[0][:, 128 * sub:128 * (sub + 1)]
                    dfv = scr[1][:, 128 * sub:128 * (sub + 1)]
                    sqv = scr[2][:, 128 * sub:128 * (sub + 1)]
                    onv = (scr[3][:, 128 * sub:128 * (sub + 1)]
                           .bitcast(BF16)[:, 0:128])
                    V.op("reciprocal", ep_st[:, c0:c0 + 1],
                         ps_big[:, o1 + 128:o1 + 129],
                         waits=[m_av_done, m_lam] if sub == 0 else None)
                    V.op("reciprocal", ep_st[:, c0 + 1:c0 + 2],
                         ps_big[:, o2 + 128:o2 + 129])
                    V.op("tensor_tensor", ep_st[:, c0 + 2:c0 + 3],
                         ep_st[:, c0 + 1:c0 + 2],
                         lam_sb[:, 0:1], ALU.mult)
                    V.op("tensor_scalar", t2v,
                         ps_big[:, o2:o2 + 128],
                         ep_st[:, c0 + 2:c0 + 3], None, ALU.mult)
                    m_diff = V.op("scalar_tensor_tensor", dfv,
                                  ps_big[:, o1:o1 + 128],
                                  ep_st[:, c0:c0 + 1], t2v, ALU.mult,
                                  ALU.subtract)
                    S.op("activation", sqv, dfv, AF.Square,
                         accum_out=ep_st[:, c0 + 3:c0 + 4], waits=[m_diff])
                    S.op("activation", ep_st[:, c0 + 4:c0 + 5],
                         ep_st[:, c0 + 3:c0 + 4], AF.Ln,
                         bias=EPS, scale=1.0 / VD)
                    m_rs = S.op("activation", ep_st[:, c0 + 5:c0 + 6],
                                ep_st[:, c0 + 4:c0 + 5],
                                AF.Exp, bias=math.log(ONE_M_LI), scale=-0.5)
                    eps.append((m_rs, dfv, onv, c0))
                for sub in range(2):
                    m_rs, dfv, onv, c0 = eps[sub]
                    stp = sub % 2
                    m_on = V.op("tensor_scalar", onv, dfv,
                                ep_st[:, c0 + 5:c0 + 6], None, ALU.mult,
                                waits=[m_rs])
                    bank = [ps_b, ps_d][stp]
                    dstps = bank[:, 384:448].bitcast(BF16)
                    TE.wait(m_on)
                    TE.wait(tr_bank_war[stp])
                    m_tr = TE.mile("transpose", dstps, onv, id_b[:])
                    m_cp = V.op("tensor_copy", att_st[stp][:], dstps,
                                waits=[m_tr, att_sp[stp]])
                    qs = 2 * jt + sub
                    att_sp[stp] = Dsc.dma("sp_att",
                                          attnT_d.ap()[qs, :, h, :],
                                          att_st[stp][:], waits=[m_cp])
                    ep_copies.append(m_cp)
                    tr_bank_war[stp] = m_cp
                emit_adq()
            hd_war[hb] = TE.fence()
        while ad_state["i2"] < 128:
            emit_adq()

        gu_sc_sb = v8(AR2, 16384, 32768, BF16, NF)
        m_gusc = Dgp.dma("ld_gusc", gu_sc_sb, gu_sc.ap(),
                         waits=_aslist([hd_war[0], hd_war[1]]))

        m_att_all = p.mark("sp_att")
        aq_war = [None, None]
        sev_war = None
        m_sp_s = None
        prev_guard = list(ep_copies)
        hd_fence = TE.fence()
        for qs in range(8):
            bufp = qs % 2
            m_aq = Dsy.dma(f"ld_aq{bufp}",
                           att_qs[bufp].rearrange("p (a f) -> p a f", a=16),
                           attnT_d.ap()[qs],
                           waits=[m_att_all] + (hd_fence if qs < 2 else [])
                                 + _aslist(aq_war[bufp]))
            TE.wait(m_aq)
            if qs == 0:
                TE.wait(m_wp)
            for mk in prev_guard:
                TE.wait(mk)
            m_pj = None
            for h in range(NH):
                for c4 in range(4):
                    kw = dict(start=(h == 0), stop=(h == 15))
                    lhsT = att_qs[bufp][:, 128 * h:128 * (h + 1)]
                    rhs = wk_sb[:, 2048 * h + 512 * c4:2048 * h + 512 * (c4 + 1)]
                    if (h, c4) == (15, 3):
                        m_pj = TE.mile("matmul",
                                       ps_big[:, 512 * c4:512 * (c4 + 1)],
                                       lhsT, rhs, **kw)
                    else:
                        TE.mm(ps_big[:, 512 * c4:512 * (c4 + 1)], lhsT, rhs,
                              **kw)
            aq_war[bufp] = TE.fence()
            m_ev = V.op("tensor_copy", s_ev[0], ps_big[:],
                        waits=[m_pj, sev_war])
            m_sp_s = Dsc.dma("sp_s", s_d.ap()[128 * qs:128 * (qs + 1), :],
                             s_ev[0], waits=[m_ev])
            sev_war = m_sp_s
            prev_guard = [m_ev]
        m_proj_ev = prev_guard[0]

        m_p5_te = TE.fence()
        mb_v = V.mark()
        mb_s = S.mark()
        m_spwd = [p.mark("sp_wd0a"), p.mark("sp_wd0b"),
                  p.mark("sp_wd1a"), p.mark("sp_wd1b")]
        m_scD = [p.mark("ld_scD0"), p.mark("ld_scD1")]

        xnqA = v8(AR1, 0, 32768, BF16, 16 * NQ)
        glA = [v8(AR1, 32768 + 4096 * i, 4096, F32, NQ) for i in range(4)]
        c4A = [v8(AR3, 16384, 2048, I16, 1024),
               v8(AR1, 49152, 2048, I16, 1024)]
        s15A = [v8(AR3, 18432, 2048, I16, 1024),
                v8(AR1, 51200, 2048, I16, 1024)]
        rA = [v8(AR3, 20480, 2048, BF16, 1024),
              v8(AR1, 61440, 2048, BF16, 1024)]
        scb2A = [v8(AR1, 53248 + 1024 * i, 1024, BF16, 512) for i in range(4)]
        ievA = [v8(AR1, 57344, 2048, BF16, NQ), v8(AR1, 59392, 2048, BF16, NQ)]
        pkA = [[v8(AR3, 0, 4096, U8, 4096), v8(AR3, 4096, 4096, U8, 4096)],
               [v8(AR3, 8192, 4096, U8, 4096), v8(AR3, 12288, 4096, U8, 4096)]]
        gubuf = [[v8(AR0, 0, 16384, BF16, 8192),
                  v8(AR0, 16384, 16384, BF16, 8192)],
                 [v8(AR0, 32768, 16384, BF16, 8192),
                  v8(AR0, 49152, 16384, BF16, 8192)]]
        gubuf_i = [[v8(AR0, 0, 16384, I16, 8192),
                    v8(AR0, 16384, 16384, I16, 8192)],
                   [v8(AR0, 32768, 16384, I16, 8192),
                    v8(AR0, 49152, 16384, I16, 8192)]]

        for g in range(16):
            J2, par2 = divmod(g, 2)
            src_ = xnqT_d.ap()[256 * J2 + par2:256 * J2 + 256:2, 0:NQ]
            w0 = [p.mark("sp_xnqT"), m_msk_last] if g == 0 else []
            Dsy.dma("ld_xnq", xnqA[:, NQ * g:NQ * (g + 1)], src_, waits=w0)
        m_xnq = p.mark("ld_xnq")

        pmA = [dict(xb=[mb_v, mb_s], orr=mb_v, mul1=mb_v, val=m_spwd),
               dict(xb=[mb_v, mb_s], orr=mb_v, mul1=mb_v, val=m_spwd)]
        guready = {}
        fenceA = {}
        pkA_marks = {}
        ispill = {}
        ievA_war = [[mb_v] + m_spwd, [mb_v] + m_spwd]
        psA_war = {0: [mb_v, mb_s], 1: [mb_v, mb_s]}
        sqA_war = [[mb_v] + m_scD]

        scparA = [[mb_v, mb_v], [mb_v, mb_v]]

        def ffnA_s1(s, u):
            mat, J = u % 2, u // 2
            pm = pmA[mat]
            dbuf = s % 2
            pk = pkA[mat][dbuf][:, 512 * J:512 * (J + 1)]
            c4, s15 = c4A[mat], s15A[mat]
            scb2 = scb2A[2 * mat + J % 2]
            src_sc = (gu_sc_sb[64 * mat + 8 * J:64 * mat + 8 * J + 8,
                               512 * s:512 * (s + 1)]
                      .unsqueeze(1).broadcast_to([8, 16, 512]))
            m_sc = Dsc.dma(f"ld_scA{mat}{J % 2}", scb2, src_sc,
                           waits=[scparA[mat][J % 2],
                                  m_gusc if s == 0 and J == 0 else None])
            m_chi = S.op("activation", c4[:, 512:1024], pk, AF.Identity,
                         scale=0.0625, bias=-0.499999,
                         waits=[pm.get("xb"), pkA_marks[(mat, dbuf)]])
            m_clo = V.op("scalar_tensor_tensor", c4[:, 0:512], c4[:, 512:1024],
                         -16.0, pk, ALU.mult, ALU.add, noserial=True,
                         waits=[m_chi, pm.get("xb")])
            m_s15 = V.op("tensor_scalar", s15, c4, 3, 15,
                         ALU.logical_shift_right, ALU.logical_shift_left,
                         noserial=True, waits=[m_clo, pm.get("orr")])
            m_and = V.op("tensor_single_scalar", c4, c4, 7, ALU.bitwise_and,
                         noserial=True, waits=[m_s15])
            xb_i = gubuf_i[mat][dbuf][:, 1024 * J:1024 * (J + 1)]
            wl = [m_and] + _aslist(fenceA.get(s - 2, m_p5_te))
            m_xb = V.op("tensor_scalar", xb_i, c4, 64, 16128,
                        ALU.mult, ALU.add, noserial=True, waits=wl)
            pm.update(sc=m_sc, chi=m_chi, clo=m_clo, s15=m_s15, xb=m_xb)

        def ffnA_s2(s, u):
            mat, J = u % 2, u // 2
            pm = pmA[mat]
            dbuf = s % 2
            r, s15 = rA[mat], s15A[mat]
            scb2 = scb2A[2 * mat + J % 2]
            xb_i = gubuf_i[mat][dbuf][:, 1024 * J:1024 * (J + 1)]
            xb_bf = gubuf[mat][dbuf][:, 1024 * J:1024 * (J + 1)]
            m_r = S.op("activation", r, xb_i, AF.Relu, bias=63.5,
                       scale=-1.0 / 256.0, waits=[pm["xb"], pm.get("val")])
            m_val = V.op("tensor_tensor", xb_bf, xb_bf, r, ALU.subtract,
                         noserial=True, waits=[m_r])
            m_or = V.op("tensor_tensor", xb_i, xb_i, s15, ALU.add,
                        noserial=True, waits=[m_val])
            m_m0 = V.op("tensor_tensor", xb_bf[:, 0:512], xb_bf[:, 0:512],
                        scb2, ALU.mult, noserial=True,
                        waits=[m_or, pm["sc"]])
            m_m1 = V.op("tensor_tensor", xb_bf[:, 512:1024],
                        xb_bf[:, 512:1024], scb2, ALU.mult, noserial=True,
                        waits=[m_m0])
            pm.update(r=m_r, val=m_val, orr=m_or, mul1=m_m1)
            scparA[mat][J % 2] = m_m1
            guready.setdefault(s, []).append(m_m1)

        unitsA = []
        dqA = {"i1": 0, "i2": 0}

        def emitA_unit():
            if dqA["i1"] < len(unitsA):
                ffnA_s1(*unitsA[dqA["i1"]])
                dqA["i1"] += 1
            lim = dqA["i1"] - (1 if dqA["i1"] < len(unitsA) else 0)
            if dqA["i2"] < lim:
                ffnA_s2(*unitsA[dqA["i2"]])
                dqA["i2"] += 1

        def emitA_chunk(s, ch):
            dbuf = s % 2
            par = ch % 2
            C = 4 * s + ch
            if ch == 0:
                for mk in guready[s]:
                    TE.wait(mk)
                TE.wait(m_xnq)
            for mk in _aslist(psA_war[par]):
                TE.wait(mk)
            m_g = m_u2 = None
            for mat in range(2):
                for th in range(2):
                    for g in range(16):
                        kw = dict(start=(g == 0), stop=(g == 15))
                        off = 1024 * (g // 2) + 512 * (g % 2) + 128 * ch
                        lhsT = gubuf[mat][dbuf][:, off:off + 128]
                        rhs = xnqA[:, NQ * g + 512 * th:NQ * g + 512 * (th + 1)]
                        if par == 0:
                            dst = [ps_ab, ps_cd][mat][:, 512 * th:512 * (th + 1)]
                        else:
                            dst = ps_big[:, 1024 * mat + 512 * th:
                                         1024 * mat + 512 * (th + 1)]
                        if (th, g) == (1, 15):
                            mk = TE.mile("matmul", dst, lhsT, rhs, **kw)
                            if mat == 0:
                                m_g = mk
                            else:
                                m_u2 = mk
                        else:
                            TE.mm(dst, lhsT, rhs, **kw)
            pgf = ps_ab[:, 0:1024] if par == 0 else ps_big[:, 0:1024]
            puf = ps_cd[:, 0:1024] if par == 0 else ps_big[:, 1024:2048]
            m_sq2 = S.op("activation", glA[0], pgf, AF.Square, scale=SQ_GC3,
                         waits=[m_g] + _aslist(sqA_war[0]))
            m_v3 = V.op("scalar_tensor_tensor", glA[2], glA[0], GC1, pgf,
                        ALU.add, ALU.mult, waits=[m_sq2])
            m_t = S.op("activation", glA[3], glA[2], AF.Tanh, waits=[m_v3])
            m_up = S.op("activation", glA[0], puf, AF.Identity, scale=0.5,
                        waits=[m_u2, m_v3])
            m_pm = V.op("tensor_tensor", glA[1], pgf, glA[0], ALU.mult,
                        waits=[m_up])
            m_int = V.op("scalar_tensor_tensor", ievA[ch % 2], glA[3], 1.0,
                         glA[1], ALU.add, ALU.mult,
                         waits=[m_t, ievA_war[ch % 2]])
            m_isp = Dsy.dma(f"sp_int{ch % 2}", inter_d.ap()[C], ievA[ch % 2],
                            waits=[m_int])
            ispill[C] = m_isp
            ievA_war[ch % 2] = m_isp
            psA_war[par] = [m_pm, m_up]
            sqA_war[0] = m_pm

        seedA = [mb_v, mb_s] + m_spwd
        pkA_war = [[seedA, seedA], [seedA, seedA]]
        for it in range(33):
            s = it
            prev = it - 1
            if s < 32:
                dbuf = s % 2
                for mat in range(2):
                    pkA_marks[(mat, dbuf)] = Dsy.dma(
                        f"ld_pkA{mat}{dbuf}", pkA[mat][dbuf],
                        [gate_pk3, up_pk3][mat].ap()[s],
                        waits=_aslist(pkA_war[mat][dbuf]))
                unitsA.extend((s, u) for u in range(16))
            for step in range(4):
                for _ in range(4):
                    emitA_unit()
                if prev >= 0:
                    emitA_chunk(prev, step)
            if s < 32:
                for mat in range(2):
                    pkA_war[mat][s % 2] = [pmA[mat]["chi"], pmA[mat]["clo"]]
            if prev >= 0:
                fenceA[prev] = TE.fence()
        while dqA["i2"] < len(unitsA):
            emitA_unit()
        mA_v = V.mark()
        mA_s = S.mark()
        m_scA_done = [p.mark(f"ld_scA{m2}{pq}")
                      for m2 in range(2) for pq in range(2)]

        wdbB = [v8(AR1, 8192 * i, 8192, BF16, 4096) for i in range(4)]
        ibufB = [v8(AR1, 32768 + 4096 * i, 4096, BF16, 2048) for i in range(4)]
        finxB = [v8(AR0, 0, 4096, F32, 1024), v8(AR0, 4096, 4096, F32, 1024)]
        finsB = [v8(AR0, 8192, 4096, F32, 1024),
                 v8(AR0, 12288, 4096, F32, 1024)]
        finlB = [v8(AR0, 16384, 4096, F32, 1024),
                 v8(AR0, 20480, 4096, F32, 1024)]

        fenceB = {}
        m_arB = ([mA_v, mA_s] + m_scD + _aslist(fenceA[31])
                 + _aslist(fenceA.get(30)))
        psq_war = [m_arB] * 4
        fin_war = [[None, None], [None, None], [None, None]]
        for pi in range(4):
            hh, tg = pi // 2, pi % 2
            m_blast = None
            for b in range(32):
                bg = 32 * pi + b
                dbuf = bg % 4
                wl = wdsp[(b, hh)] + _aslist(fenceB.get(bg - 4))
                if pi == 0 and b < 4:
                    wl += m_arB
                m_wd = Dsy.dma(f"ld_wdB{dbuf}", wdbB[dbuf],
                               wdT_dram.ap()[b, :, 4096 * hh:4096 * (hh + 1)],
                               waits=wl)
                ibm = []
                for g in range(4):
                    wl = [ispill[4 * b + g]] if pi == 0 else []
                    if g == 0:
                        wl = wl + _aslist(fenceB.get(bg - 4))
                        if pi == 0 and b < 4:
                            wl = wl + [mA_v]
                    ibm.append(Dsc.dma(
                        f"ld_intB{dbuf}",
                        ibufB[dbuf][:, 512 * g:512 * (g + 1)],
                        inter_d.ap()[4 * b + g, :, 512 * tg:512 * (tg + 1)],
                        waits=wl))
                TE.wait(m_wd)
                for mk in ibm:
                    TE.wait(mk)
                if b == 0:
                    for mk in _aslist(psq_war):
                        TE.wait(mk)
                for tc4 in range(4):
                    for g in range(4):
                        kw = dict(start=(b == 0 and g == 0),
                                  stop=(b == 31 and g == 3))
                        lhsT = ibufB[dbuf][:, 512 * g + 128 * tc4:
                                           512 * g + 128 * (tc4 + 1)]
                        for hq in range(2):
                            rhs = wdbB[dbuf][:, 1024 * g + 512 * hq:
                                             1024 * g + 512 * (hq + 1)]
                            if tc4 < 2:
                                dst = [ps_ab, ps_cd][tc4][:, 512 * hq:
                                                          512 * (hq + 1)]
                            else:
                                dst = ps_big[:, 1024 * (tc4 - 2) + 512 * hq:
                                             1024 * (tc4 - 2) + 512 * (hq + 1)]
                            if (tc4, g, hq) == (3, 3, 1):
                                m_blast = TE.mile("matmul", dst, lhsT, rhs,
                                                  **kw)
                            else:
                                TE.mm(dst, lhsT, rhs, **kw)
                fenceB[bg] = TE.fence()
            nw = []
            for tc4 in range(4):
                tc = 4 * tg + tc4
                fp = tc4 % 2
                psq = [ps_ab[:, 0:1024], ps_cd[:, 0:1024],
                       ps_big[:, 0:1024], ps_big[:, 1024:2048]][tc4]
                wl = [fin_war[0][fp]]
                if pi == 0 and tc4 < 2:
                    wl += m_arB
                m_fx = Dsy.dma(f"ld_fx{fp}", finxB[fp],
                               x_q.ap()[128 * tc:128 * (tc + 1),
                                        1024 * hh:1024 * (hh + 1)], waits=wl)
                m_fs = Dsy.dma(f"ld_fs{fp}", finsB[fp],
                               s_d.ap()[128 * tc:128 * (tc + 1),
                                        1024 * hh:1024 * (hh + 1)],
                               waits=[m_sp_s if (pi == 0 and tc4 == 0) else None,
                                      fin_war[1][fp]] +
                                     (m_arB if pi == 0 and tc4 < 2 else []))
                wl = [m_blast, fin_war[2][fp]]
                if pi == 0 and tc4 < 2:
                    wl += m_arB
                m_sil = S.op("activation", finlB[fp], psq, AF.Silu, waits=wl)
                m_o1 = V.op("tensor_tensor", finlB[fp], finlB[fp], finsB[fp],
                            ALU.mult, waits=[m_sil, m_fs])
                m_o2 = V.op("tensor_tensor", finlB[fp], finlB[fp], finxB[fp],
                            ALU.add, waits=[m_fx])
                m_out = Dsc.dma("sp_out",
                                out.ap()[128 * tc:128 * (tc + 1),
                                         1024 * hh:1024 * (hh + 1)],
                                finlB[fp], waits=[m_o2])
                fin_war[0][fp] = m_o2
                fin_war[1][fp] = m_o1
                fin_war[2][fp] = [m_o1, m_out]
                nw.append(m_sil)
            psq_war = nw

        p.wait("sync", p.mark("sp_out"))

        with nc.Block() as block:
            @block.tensor
            def _(eng):
                for f in p.body["tensor"]:
                    f(eng)

            @block.vector
            def _(eng):
                for f in p.body["vector"]:
                    f(eng)

            @block.scalar
            def _(eng):
                for f in p.body["scalar"]:
                    f(eng)

            @block.gpsimd
            def _(eng):
                for f in p.body["gpsimd"]:
                    f(eng)

            @block.sync
            def _(eng):
                for f in p.body["sync"]:
                    f(eng)

        res = run_bass_kernel_spmd(nc, in_maps, core_ids=list(range(N_CORES)),
                                   trace=trace)
    return res


def kernel(**inputs):
    in_maps = _host_inputs(inputs)
    res = build_and_run(in_maps)
    full = np.empty((2, T, D), np.float32)
    for c in range(N_CORES):
        b, q = divmod(c, 4)
        oc = np.asarray(res.results[c]["out"])
        for jt, t in enumerate(TILES[q]):
            full[b, 256 * t:256 * (t + 1)] = oc[256 * jt:256 * (jt + 1)]
    return full
